# version 11
# baseline (speedup 1.0000x reference)
import numpy as np
import concourse.bass as bass
import concourse.mybir as mybir
from concourse.bass_utils import run_bass_kernel_spmd

F32 = mybir.dt.float32
BF16 = mybir.dt.bfloat16
AF = mybir.ActivationFunctionType
OP = mybir.AluOpType
AX = mybir.AxisListType

NCORE = 8
BPC = 2
S = 2048
D = 1024
C = 16
T = BPC * S
E = 8
FF = 1408
NSEG = 10
FK = FF // 128
SCALE = 1.0 / 32.0
MOE_SCALE = 2.5
EPS = 1e-5

SBUF_TOP = 229344
MAP_BYTES = 11264 + 196608
SB0 = SBUF_TOP - MAP_BYTES
PB = SB0 + 11264

PERM = dict(
    id32=(0 + SB0, [128, 128], F32),
    iota_r=(512 + SB0, [128, 128], F32),
    iota_c=(1024 + SB0, [128, 1], F32),
    gw_nat=(1056 + SB0, [8, 1024], F32),
    gwT=(5152 + SB0, [128, 64], F32),
    stats=(5408 + SB0, [128, 128], F32),
    gstats=(5920 + SB0, [128, 64], F32),
    lgbuf=(6176 + SB0, [128, 128], F32),
    comb=(6688 + SB0, [128, 128], F32),
    gtmp1=(7200 + SB0, [128, 128], F32),
    gtmp2=(7712 + SB0, [128, 128], F32),
    getmp=(8224 + SB0, [128, 128], F32),
    h2p32=(8736 + SB0, [128, 256], F32),
    gwb=(8736 + SB0, [128, 64], BF16),
    zbuf=(9760 + SB0, [1, 640], BF16),
)

ATTN = dict(
    A1=(PB + 0, [128, 16384], BF16),
    A2=(PB + 32768, [128, 16384], BF16),
    h_T=(PB + 65536, [128, 16384], BF16),
    p_row0=(PB + 65536, [128, 2048], BF16),
    p_row1=(PB + 69632, [128, 2048], BF16),
    ptsG0=(PB + 73728, [128, 4096], BF16),
    ptsG1=(PB + 81920, [128, 4096], BF16),
    unatc0=(PB + 90112, [128, 1024], BF16),
    unatc1=(PB + 92160, [128, 1024], BF16),
    vnatc0=(PB + 94208, [128, 1024], BF16),
    vnatc1=(PB + 96256, [128, 1024], BF16),
    UT=(PB + 98304, [128, 8192], BF16),
    VT=(PB + 114688, [128, 8192], BF16),
    U_nat=(PB + 131072, [128, 8192], BF16),
    V_nat=(PB + 147456, [128, 8192], BF16),
    x_in0=(PB + 163840, [128, 1024], F32),
    x_in1=(PB + 167936, [128, 1024], F32),
    x_in2=(PB + 172032, [128, 1024], F32),
    x_in3=(PB + 176128, [128, 1024], F32),
    sq=(PB + 180224, [128, 1024], F32),
    h_t0=(PB + 184320, [128, 1024], BF16),
    h_t1=(PB + 186368, [128, 1024], BF16),
    xp_out0=(PB + 188416, [128, 1024], F32),
    xp_out1=(PB + 192512, [128, 1024], F32),
)

FFN = dict(
    y_acc=(PB + 0, [128, 16384], F32),
    h2T=(PB + 65536, [128, 16384], BF16),
    m_T=(PB + 98304, [128, 22528], BF16),
    wd=(PB + 143360, [128, 11264], BF16),
    wgr0=(PB + 165888, [128, 1024], BF16),
    wgr1=(PB + 167936, [128, 1024], BF16),
    wur0=(PB + 169984, [128, 1024], BF16),
    wur1=(PB + 172032, [128, 1024], BF16),
    silu0=(PB + 174080, [128, 1024], BF16),
    silu1=(PB + 176128, [128, 1024], BF16),
    h2f0=(PB + 178176, [128, 1024], F32),
    h2f1=(PB + 182272, [128, 1024], F32),
    fsq=(PB + 186368, [128, 1024], F32),
    fh2b0=(PB + 190464, [128, 1024], BF16),
    fh2b1=(PB + 192512, [128, 1024], BF16),
)


class Sem:
    __slots__ = ("name", "n", "h")

    def __init__(self, name):
        self.name, self.n, self.h = name, 0, None


class Plan:
    ENGS = ("tensor", "vector", "scalar", "gpsimd", "sync")

    def __init__(self):
        self.ops = {e: [] for e in self.ENGS}
        self.sems = []
        self._waited = {e: {} for e in self.ENGS}

    def sem(self, name):
        s = Sem(name)
        self.sems.append(s)
        return s

    def op(self, eng, build, waits=(), inc=None, amt=1, tag=""):
        wl = []
        seen = self._waited[eng]
        for (s, v) in waits:
            if v is None or v <= 0:
                continue
            if seen.get(s.name, -1) >= v:
                continue
            seen[s.name] = v
            wl.append((s, v))
        if inc is not None:
            inc.n += amt
        post = inc.n if inc else None

        def f(e, _b=build, _w=wl, _i=inc, _a=amt):
            for s, v in _w:
                e.wait_ge(s.h, v)
            if _b is not None:
                ins = _b(e)
                if _i is not None:
                    ins.then_inc(_i.h, _a)

        f.tag = tag
        f.waits = wl
        f.inc = (inc, amt) if inc else None
        self.ops[eng].append(f)
        return post

    def check(self):
        cnt = {s.name: 0 for s in self.sems}
        idx = {e: 0 for e in self.ENGS}
        progress = True
        while progress:
            progress = False
            for e in self.ENGS:
                while idx[e] < len(self.ops[e]):
                    f = self.ops[e][idx[e]]
                    if all(cnt[s.name] >= v for s, v in f.waits):
                        if f.inc:
                            cnt[f.inc[0].name] += f.inc[1]
                        idx[e] += 1
                        progress = True
                    else:
                        break
        stuck = {e: len(self.ops[e]) - idx[e] for e in self.ENGS if idx[e] < len(self.ops[e])}
        if stuck:
            msg = []
            for e in stuck:
                f = self.ops[e][idx[e]]
                blocked = [(s.name, v, cnt[s.name]) for s, v in f.waits if cnt[s.name] < v]
                msg.append(f"{e}[{idx[e]}] tag={f.tag} blocked on {blocked}")
            raise RuntimeError("plan deadlock:\n" + "\n".join(msg))


def _build_program(do_attn=True, do_ffn=True, heads=C, nseg=NSEG, debug_xp_out=False,
                   bmax=BPC, do_proj=True, do_g=True, dump=None, ffn_halves=BPC):
    nc = bass.Bass("TRN2", target_bir_lowering=False)

    x_t = nc.dram_tensor("x", [T, D], F32, kind="ExternalInput")
    u_t = nc.dram_tensor("u", [D, D], F32, kind="ExternalInput")
    v_t = nc.dram_tensor("v", [D, D], F32, kind="ExternalInput")
    gw_t = nc.dram_tensor("gw", [E, D], F32, kind="ExternalInput")
    wg_t = nc.dram_tensor("wg", [E, D, FF], F32, kind="ExternalInput")
    wu_t = nc.dram_tensor("wu", [E, D, FF], F32, kind="ExternalInput")
    wd_t = nc.dram_tensor("wd", [E, FF, D], F32, kind="ExternalInput")
    swg_t = nc.dram_tensor("swg", [D, 2 * FF], F32, kind="ExternalInput")
    swu_t = nc.dram_tensor("swu", [D, 2 * FF], F32, kind="ExternalInput")
    swd_t = nc.dram_tensor("swd", [2 * FF, D], F32, kind="ExternalInput")
    out_t = nc.dram_tensor("out", [T, D], F32, kind="ExternalOutput")
    xp_t = nc.dram_tensor("xp", [T, D], F32)

    sb = {}
    for table in (PERM, ATTN, FFN):
        for name, (off, shape, dt) in table.items():
            sb[name] = nc.alloc_sbuf_tensor_at(name, shape, dt, offset=off)

    psA = nc.alloc_psum_tensor("psA", [128, 2048], F32)
    psB = nc.alloc_psum_tensor("psB", [128, 2048], F32)

    P = Plan()
    dx = [P.sem(f"dx{j}") for j in range(4)]
    dy = [P.sem(f"dy{j}") for j in range(8)]
    dwg = [P.sem(f"dwg{j}") for j in range(2)]
    dwu = [P.sem(f"dwu{j}") for j in range(2)]
    dwd = P.sem("dwd")
    duv = P.sem("duv")
    dgw = P.sem("dgw")
    duvt = P.sem("duvt")
    dht = [P.sem(f"dht{j}") for j in range(2)]
    dnat = [P.sem(f"dnat{j}") for j in range(4)]
    dstr = [P.sem(f"dstr{j}") for j in range(4)]
    dsp = [P.sem(f"dsp{j}") for j in range(2)]
    dout = P.sem("dout")
    st = P.sem("st")
    sv = P.sem("sv")
    ss = P.sem("ss")
    sg = P.sem("sg")

    x = x_t.ap()
    xp = xp_t.ap()

    P.op("gpsimd", lambda e: e.iota(sb["iota_r"][:, :], [[1, 128]], channel_multiplier=0,
                                    allow_small_or_imprecise_dtypes=True), inc=sg, tag="iota_r")
    P.op("gpsimd", lambda e: e.iota(sb["iota_c"][:, :], [[0, 1]], channel_multiplier=1,
                                    allow_small_or_imprecise_dtypes=True), inc=sg, tag="iota_c")
    v_iota = sg.n
    P.op("vector", lambda e: e.tensor_scalar(sb["id32"][:, :], sb["iota_r"][:, :],
                                             sb["iota_c"][:, 0:1], None, OP.is_equal),
         waits=[(sg, v_iota)], inc=sv, tag="id32")
    v_id32 = sv.n
    P.op("vector", lambda e: e.memset(sb["zbuf"][:, :], 0.0), inc=sv, tag="zbuf")
    v_zbuf = sv.n

    P.op("gpsimd", lambda e: e.dma_start(
        sb["U_nat"][:, :].rearrange("p (c d) -> p c d", c=8),
        bass.AP(u_t, 0, [[D, 128], [128 * D, 8], [1, D]])), inc=duv, amt=16, tag="dma_Unat")
    P.op("gpsimd", lambda e: e.dma_start(
        sb["V_nat"][:, :].rearrange("p (c d) -> p c d", c=8),
        bass.AP(v_t, 0, [[D, 128], [128 * D, 8], [1, D]])), inc=duv, amt=16, tag="dma_Vnat")
    v_uvnat = duv.n

    P.op("sync", lambda e: e.dma_start(sb["gw_nat"][:, :], gw_t.ap()[:, :]),
         inc=dgw, amt=16, tag="dma_gw")
    v_gwload = dgw.n

    for mat, src, dst in (("u", "U_nat", "UT"), ("v", "V_nat", "VT")):
        for j in range(8):
            def b_t(e, src=src, dst=dst, j=j):
                return e.dma_start_transpose(
                    sb[dst][:, :].rearrange("p (kk c) -> p kk c", kk=8)[:, :, j * 128:(j + 1) * 128],
                    sb[src][:, j * 1024:(j + 1) * 1024])
            P.op("sync", b_t, waits=[(duv, v_uvnat)], inc=duvt, amt=16, tag=f"dmat_{dst}{j}")
    v_utvt = duvt.n

    for kk in range(8):
        def b_gwt(e, kk=kk):
            return e.transpose(psB[:, kk * 8:(kk + 1) * 8], sb["gw_nat"][0:8, kk * 128:(kk + 1) * 128],
                               sb["id32"][0:8, 0:8])
        P.op("tensor", b_gwt, waits=[(dgw, v_gwload), (sv, v_id32)], inc=st, tag=f"tr_gw{kk}")
    v_gwtr = st.n
    P.op("scalar", lambda e: e.activation(sb["gwT"][:, 0:64], psB[:, 0:64], AF.Copy),
         waits=[(st, v_gwtr)], inc=ss, tag="ev_gwT")
    v_gwT = ss.n
    P.op("vector", lambda e: e.tensor_copy(sb["gwb"][:, :], sb["gwT"][:, :]),
         waits=[(ss, v_gwT)], inc=sv, tag="gwb")
    v_gwb = sv.n

    def ln_chain(i, src_ap_fn, sqbuf, waits_first, tagp, rstd_done):
        s0 = (i % 16) * 8
        stats = sb["stats"]
        P.op("vector", lambda e: e.reduce_sum(stats[:, s0:s0 + 1], src_ap_fn(), AX.X, negate=True),
             waits=list(waits_first), inc=sv, tag=f"{tagp}{i}_sum")
        v1 = sv.n
        yield
        P.op("vector", lambda e: e.scalar_tensor_tensor(sb[sqbuf][:, :], src_ap_fn(), 1.0,
                                                        src_ap_fn(), OP.mult, OP.mult,
                                                        accum_out=stats[:, s0 + 1:s0 + 2]),
             inc=sv, tag=f"{tagp}{i}_sq")
        v2 = sv.n
        yield
        P.op("vector", lambda e: e.tensor_scalar(stats[:, s0 + 2:s0 + 3], stats[:, s0:s0 + 1],
                                                 1.0 / D, None, OP.mult),
             waits=[(sv, v1)], inc=sv, tag=f"{tagp}{i}_negm")
        v3 = sv.n
        yield
        P.op("vector", lambda e: e.tensor_scalar(stats[:, s0 + 3:s0 + 4], stats[:, s0 + 1:s0 + 2],
                                                 1.0 / D, EPS, OP.mult, OP.add),
             waits=[(sv, v2)], inc=sv, tag=f"{tagp}{i}_q")
        v4 = sv.n
        yield
        P.op("vector", lambda e: e.tensor_scalar(stats[:, s0 + 4:s0 + 5], stats[:, s0 + 2:s0 + 3],
                                                 stats[:, s0 + 2:s0 + 3], None, OP.mult),
             waits=[(sv, v3)], inc=sv, tag=f"{tagp}{i}_m2")
        v5 = sv.n
        yield
        P.op("vector", lambda e: e.tensor_sub(stats[:, s0 + 5:s0 + 6], stats[:, s0 + 3:s0 + 4],
                                              stats[:, s0 + 4:s0 + 5]),
             waits=[(sv, v5)], inc=sv, tag=f"{tagp}{i}_ve")
        v6 = sv.n
        yield
        P.op("vector", lambda e: e.reciprocal(stats[:, s0 + 6:s0 + 7], stats[:, s0 + 5:s0 + 6]),
             waits=[(sv, v6)], inc=sv, tag=f"{tagp}{i}_ri")
        v7 = sv.n
        yield
        P.op("scalar", lambda e: e.activation(stats[:, s0 + 7:s0 + 8], stats[:, s0 + 6:s0 + 7], AF.Sqrt),
             waits=[(sv, v7)], inc=ss, tag=f"{tagp}{i}_rstd")
        rstd_done[i] = ss.n

    def run_waves(gens):
        alive = list(gens)
        while alive:
            nxt = []
            for g in alive:
                try:
                    next(g)
                    nxt.append(g)
                except StopIteration:
                    pass
            alive = nxt

    def ln_apply(i, src_ap_fn, out_ap_fn, apply_waits, tagp):
        s0 = (i % 16) * 8
        stats = sb["stats"]
        P.op("vector", lambda e: e.tensor_scalar(
            out_ap_fn(), src_ap_fn(), stats[:, s0 + 2:s0 + 3], stats[:, s0 + 7:s0 + 8],
            OP.add, OP.mult), waits=list(apply_waits), inc=sv, tag=f"{tagp}{i}_app")
        return sv.n

    xin_free = [0, 0, 0, 0]
    xin_load = [0, 0, 0, 0]
    xpout_free = [0, 0]
    v_G_end = 0
    v_Gevac_last = 0
    v_spill = [[0, 0], [0, 0]]
    v_heads_end = 0
    v_strips_last = [0, 0, 0, 0]
    v_natc_last = [0, 0, 0, 0]
    v_ht_slot = [0, 0]
    v_sig_all = 0

    pieces = [(psA, 0), (psA, 1024), (psB, 0), (psB, 1024)]

    hT_done = {}

    def emit_attn_ln_tile_tail(b, i, rstd_done, prev_heads_end, prev_strips, prev_natc):
        slot, hs = i % 4, i % 2
        ap_done = ln_apply(i, (lambda s=slot: sb[f"x_in{s}"][:, :]),
                           (lambda s=hs: sb[f"h_t{s}"][:, :]),
                           [(ss, rstd_done[i]), (dht[hs], v_ht_slot[hs])], f"lnA{b}_")
        xin_free[slot] = ap_done

        def b_ttr(e, i=i, s=hs):
            return e.dma_start_transpose(
                sb["h_T"][:, :].rearrange("p (kk t) -> p kk t", kk=8)[:, :, i * 128:(i + 1) * 128],
                sb[f"h_t{s}"][:, :])
        P.op("sync", b_ttr, waits=[(sv, ap_done), (st, prev_heads_end)]
             + [(dstr[j], prev_strips[j]) for j in range(4)]
             + [(dnat[j], prev_natc[j]) for j in range(4)],
             inc=dht[hs], amt=16, tag=f"dmat_hT_b{b}_{i}")
        v_ht_slot[hs] = dht[hs].n
        hT_done[(b, i)] = dht[hs].n

    def emit_attn_ln_x_load(b, i):
        slot = i % 4
        r0 = b * S + i * 128

        def b_ldx(e, r0=r0, slot=slot):
            return e.dma_start(sb[f"x_in{slot}"][:, :], x[r0:r0 + 128, :])
        P.op("sync", b_ldx, waits=[(sv, xin_free[slot])], inc=dx[slot], amt=16,
             tag=f"dma_x_b{b}_{i}")
        xin_load[slot] = dx[slot].n

    def emit_attn_ln_group(b, g4, prev_heads_end, prev_strips, prev_natc):
        tiles = [4 * g4 + k for k in range(4)]
        rstd_done = {}
        gens = []
        for i in tiles:
            emit_attn_ln_x_load(b, i)
            gens.append(ln_chain(i, (lambda s=i % 4: sb[f"x_in{s}"][:, :]), "sq",
                                 [(dx[i % 4], xin_load[i % 4])], f"lnA{b}_", rstd_done))
        run_waves(gens)
        for k, i in enumerate(tiles):
            emit_attn_ln_tile_tail(b, i, rstd_done, prev_heads_end, prev_strips, prev_natc)

    def emit_attn_ln_pair(b, i0, prev_heads_end, prev_strips, prev_natc):
        rstd_done = {}
        gens = []
        for i in (i0, i0 + 1):
            emit_attn_ln_x_load(b, i)
            gens.append(ln_chain(i, (lambda s=i % 4: sb[f"x_in{s}"][:, :]), "sq",
                                 [(dx[i % 4], xin_load[i % 4])], f"lnA{b}_", rstd_done))
        run_waves(gens)
        for i in (i0, i0 + 1):
            emit_attn_ln_tile_tail(b, i, rstd_done, prev_heads_end, prev_strips, prev_natc)

    if do_attn:
        for g4 in range(4):
            emit_attn_ln_group(0, g4, 0, [0, 0, 0, 0], [0, 0, 0, 0])
        for b in range(bmax):
            qpieces = [(psA, 0), (psA, 512), (psA, 1024), (psA, 1536),
                       (psB, 0), (psB, 512), (psB, 1024), (psB, 1536)]
            qpiece_free = [0] * 8
            for q in range(4):
                if not do_proj:
                    break
                for im, (wbuf, obuf) in enumerate((("UT", "A1"), ("VT", "A2"))):
                    for d_out in range(8):
                        g = q * 16 + im * 8 + d_out
                        ps, off = qpieces[g % 8]
                        first_waits = [(dht[(q * 4 + 3) % 2], hT_done[(b, q * 4 + 3)]),
                                       (dht[(q * 4 + 2) % 2], hT_done[(b, q * 4 + 2)]),
                                       (duvt, v_utvt),
                                       (ss, qpiece_free[g % 8]),
                                       (sv, v_Gevac_last), (ss, v_gwT)]
                        for kk in range(8):
                            def b_mm(e, ps=ps, off=off, kk=kk, wbuf=wbuf,
                                     d_out=d_out, q=q):
                                return e.matmul(
                                    ps[:, off: off + 512],
                                    sb[wbuf][:, kk * 1024 + d_out * 128: kk * 1024 + d_out * 128 + 128],
                                    sb["h_T"][:, kk * 2048 + q * 512: kk * 2048 + q * 512 + 512],
                                    start=(kk == 0), stop=(kk == 7))
                            P.op("tensor", b_mm,
                                 waits=first_waits if kk == 0 else (),
                                 inc=st if kk == 7 else None,
                                 tag=f"proj{b}_{g}_{kk}")
                        v_g = st.n

                        def b_ev(e, ps=ps, off=off, obuf=obuf, d_out=d_out, q=q):
                            return e.activation(
                                sb[obuf][:, d_out * 2048 + q * 512: d_out * 2048 + q * 512 + 512],
                                ps[:, off:off + 512], AF.Copy, scale=SCALE)
                        P.op("scalar", b_ev, waits=[(st, v_g)], inc=ss, tag=f"projev{b}_{g}")
                        qpiece_free[g % 8] = ss.n
            v_proj_ev = ss.n

            natc_free = [0, 0]
            natc_val = [None, None]
            oevac_prev = 0
            oevac_prev_ss = 0
            prow_free = [0, 0]
            prow_free_dt = [[], []]
            ptsg_free = [0, 0]
            fpair_free = [0, 0]

            def emit_natc(c):
                cc2 = c % 2
                vals = []
                for sidx, (srcb, dstb) in enumerate((("A1", f"unatc{cc2}"), ("A2", f"vnatc{cc2}"))):
                    def b_nat(e, srcb=srcb, dstb=dstb, base=cc2 * 64, chunk=c // 2):
                        return e.dma_start_transpose(
                            sb[dstb][:, :].rearrange("p (j r) -> p j r", j=16),
                            sb[srcb][base:base + 64, chunk * 2048:(chunk + 1) * 2048])
                    sem = dnat[sidx * 2 + cc2]
                    P.op("sync", b_nat, waits=[(ss, v_proj_ev), (st, natc_free[cc2])],
                         inc=sem, amt=16, tag=f"nat{b}_{c}")
                    vals.append((sem, sem.n))
                natc_val[cc2] = vals

            emit_natc(0)
            for c in range(heads):
                cc = c % 2
                base = cc * 64
                chunk = c // 2
                v_natc = natc_val[cc]

                v_zfill = None
                sig_done = {}
                strip_done = {}
                o1_done = {}
                o2_done = {}

                def emit_o2(i):
                    sl = i % 2
                    for tq in range(4):
                        w = [(ss, sig_done[(i, tq // 2)])]
                        if i == 0 and tq == 0:
                            w += [(st, v_zfill)] + list(v_natc)

                        def b_o2(e, sl=sl, tq=tq, i=i, cc=cc):
                            return e.matmul(
                                psB[64:128, tq * 512: tq * 512 + 512],
                                sb[f"unatc{cc}"][:, i * 64: i * 64 + 64],
                                sb[f"p_row{sl}"][:, tq * 512: tq * 512 + 512],
                                start=False, stop=(i == 15), tile_position=(0, 64),
                                skip_group_check=True)
                        P.op("tensor", b_o2, waits=w, inc=st if tq == 3 else None,
                             tag=f"o2_{b}_{c}_{i}_{tq}")
                    o2_done[i] = st.n
                    prow_free[sl] = st.n

                def emit_o1(g):
                    for j in range(16):
                        i1 = 2 * g + 1
                        w = [(dstr[i1 % 4], strip_done[i1]),
                             (dstr[(i1 - 1) % 4], strip_done[i1 - 1])]
                        if g == 0 and j == 0:
                            w += [(st, v_zfill)] + list(v_natc)

                        def b_o1(e, g=g, j=j, cc=cc):
                            return e.matmul(
                                psB[0:64, g * 256: g * 256 + 256],
                                sb[f"vnatc{cc}"][:, j * 64: j * 64 + 64],
                                sb[f"ptsG{g % 2}"][:, j * 256: j * 256 + 256],
                                start=False, stop=(j == 15), tile_position=(0, 0),
                                skip_group_check=True)
                        P.op("tensor", b_o1, waits=w, inc=st if j == 15 else None,
                             tag=f"o1_{b}_{c}_{g}_{j}")
                    o1_done[g] = st.n
                    ptsg_free[g % 2] = st.n

                for i in range(16):
                    sl = i % 2
                    for tq in range(4):
                        h = tq // 2

                        def b_F(e, h=h, tq=tq, base=base, chunk=chunk, i=i):
                            return e.matmul(
                                psA[:, h * 1024 + (tq % 2) * 512: h * 1024 + (tq % 2) * 512 + 512],
                                sb["A1"][base:base + 64, chunk * 2048 + i * 128: chunk * 2048 + i * 128 + 128],
                                sb["A2"][base:base + 64, chunk * 2048 + tq * 512: chunk * 2048 + tq * 512 + 512],
                                start=True, stop=True, tile_position=(base, 0))
                        P.op("tensor", b_F,
                             waits=[(ss, fpair_free[h]), (ss, v_proj_ev)],
                             inc=st if tq % 2 == 1 else None, tag=f"F{b}_{c}_{i}_{tq}")
                        if tq % 2 == 1:
                            v_fpair = st.n
                            def b_sig(e, h=h, sl=sl):
                                return e.activation(
                                    sb[f"p_row{sl}"][:, h * 1024: h * 1024 + 1024],
                                    psA[:, h * 1024: h * 1024 + 1024], AF.Sigmoid)
                            P.op("scalar", b_sig,
                                 waits=[(st, v_fpair), (st, prow_free[sl])]
                                       + prow_free_dt[sl],
                                 inc=ss, tag=f"sig{b}_{c}_{i}_{h}")
                            sig_done[(i, h)] = ss.n
                            fpair_free[h] = ss.n

                    if i == 1:
                        for zq in range(4):
                            def b_z(e, zq=zq):
                                return e.matmul(psB[:, zq * 512: zq * 512 + 512],
                                                sb["zbuf"][0:1, 0:128], sb["zbuf"][0:1, 128:640],
                                                start=True, stop=False, skip_group_check=True)
                            P.op("tensor", b_z, waits=[(sv, oevac_prev), (ss, oevac_prev_ss),
                                                       (sv, v_zbuf), (ss, v_proj_ev)],
                                 inc=st if zq == 3 else None, tag=f"zfill{b}_{c}_{zq}")
                        v_zfill = st.n
                    if i == 8 and c + 1 < heads:
                        emit_natc(c + 1)
                    if i >= 1:
                        emit_o2(i - 1)
                    if i >= 4 and i % 2 == 0:
                        emit_o1((i - 4) // 2)

                    g = i // 2

                    def b_strip(e, sl=sl, g=g, i=i):
                        return e.dma_start_transpose(
                            sb[f"ptsG{g % 2}"][:, :]
                            .rearrange("p (j s) -> p j s", j=16)[:, :, (i % 2) * 128:(i % 2) * 128 + 128],
                            sb[f"p_row{sl}"][:, :])
                    P.op("sync", b_strip, waits=[(ss, sig_done[(i, 1)]),
                                                 (st, ptsg_free[g % 2])],
                         inc=dstr[i % 4], amt=16, tag=f"strip{b}_{c}_{i}")
                    strip_done[i] = dstr[i % 4].n
                    prow_free_dt[sl] = [(dstr[i % 4], dstr[i % 4].n)]


                emit_o2(15)
                emit_o1(6)
                emit_o1(7)
                v_oend = max(o2_done[15], o1_done[7])
                v_heads_end = st.n
                natc_free[cc] = st.n

                def b_ev1(e, base=base, chunk=chunk):
                    return e.tensor_copy(sb["A1"][base:base + 64, chunk * 2048:(chunk + 1) * 2048],
                                         psB[0:64, :])
                P.op("vector", b_ev1, waits=[(st, v_oend)] + list(v_natc), inc=sv,
                     tag=f"oev1_{b}_{c}")

                def b_ev2(e, base=base, chunk=chunk):
                    return e.tensor_copy(sb["A2"][base:base + 64, chunk * 2048:(chunk + 1) * 2048],
                                         psB[64:128, :])
                P.op("vector", b_ev2, inc=sv, tag=f"oev2_{b}_{c}")
                oevac_prev = sv.n
                oevac_prev_ss = 0
            v_oevac_all = sv.n
            v_strips_last = [s.n for s in dstr]
            v_natc_last = [s.n for s in dnat]
            v_sig_all = ss.n

            gpiece_free = [0, 0, 0, 0]
            for t in range(16 if do_g else 0):
                ps, off = pieces[t % 4]
                slot = t % 4
                r0 = b * S + t * 128

                def b_ldx2(e, r0=r0, slot=slot):
                    return e.dma_start(sb[f"x_in{slot}"][:, :], x[r0:r0 + 128, :])
                P.op("sync", b_ldx2, waits=[(sv, xin_free[slot])], inc=dx[slot], amt=16,
                     tag=f"dma_xG_b{b}_{t}")
                xin_load[slot] = dx[slot].n

                first_w = [(sv, v_oevac_all), (ss, oevac_prev_ss), (duv, v_uvnat),
                           (sv, gpiece_free[t % 4]), (ss, v_proj_ev), (ss, v_sig_all)]
                for kk in range(16):
                    src_o = "A1" if kk < 8 else "A2"
                    src_w = "U_nat" if kk < 8 else "V_nat"
                    k8 = kk % 8
                    for dq in range(2):
                        def b_G(e, ps=ps, off=off, dq=dq, src_o=src_o, src_w=src_w,
                                k8=k8, t=t, kk=kk):
                            return e.matmul(
                                ps[:, off + dq * 512: off + dq * 512 + 512],
                                sb[src_o][:, k8 * 2048 + t * 128: k8 * 2048 + t * 128 + 128],
                                sb[src_w][:, k8 * 1024 + dq * 512: k8 * 1024 + dq * 512 + 512],
                                start=(kk == 0), stop=(kk == 15))
                        P.op("tensor", b_G, waits=first_w if (kk == 0 and dq == 0) else (),
                             inc=st if (kk == 15 and dq == 1) else None,
                             tag=f"G{b}_{t}_{kk}_{dq}")
                v_g = st.n
                v_G_end = v_g

                def b_gev(e, ps=ps, off=off, slot=slot, t=t):
                    return e.scalar_tensor_tensor(sb[f"xp_out{t % 2}"][:, :],
                                                  ps[:, off:off + 1024], SCALE,
                                                  sb[f"x_in{slot}"][:, :], OP.mult, OP.add)
                P.op("vector", b_gev, waits=[(st, v_g), (dx[slot], xin_load[slot]),
                                             (dsp[t % 2], xpout_free[t % 2])],
                     inc=sv, tag=f"gev{b}_{t}")
                xin_free[slot] = sv.n
                gpiece_free[t % 4] = sv.n
                v_gev = sv.n
                v_Gevac_last = v_gev

                def b_spill(e, r0=r0, t=t):
                    return e.dma_start(xp[r0:r0 + 128, :], sb[f"xp_out{t % 2}"][:, :])
                P.op("sync", b_spill, waits=[(sv, v_gev)], inc=dsp[t % 2], amt=16,
                     tag=f"spill{b}_{t}")
                xpout_free[t % 2] = dsp[t % 2].n
                v_spill[b] = [dsp[0].n, dsp[1].n]

                if b + 1 < bmax and t % 2 == 1:
                    emit_attn_ln_pair(b + 1, t - 1, v_heads_end, v_strips_last, v_natc_last)

        if debug_xp_out:
            for i in range(min(bmax * 16, T // 128)):
                def b_dbg(e, i=i):
                    return e.dma_start(out_t.ap()[i * 128:(i + 1) * 128, :], xp[i * 128:(i + 1) * 128, :])
                P.op("sync", b_dbg, waits=[(dsp[0], v_spill[bmax - 1][0]),
                                           (dsp[1], v_spill[bmax - 1][1])],
                     inc=dout, amt=16, tag=f"dbgout{i}")

    if do_ffn:
        v_out_prev = 0
        v_prev_half_st = 0
        wgr_free = [0, 0]
        wur_free = [0, 0]
        aslot_free = [0, 0]
        yslot_free = [0, 0]
        v_Bend_prev = 0
        for H in range(ffn_halves):
            w_yacc = [(st, v_G_end), (dsp[0], v_spill[H][0]), (dsp[1], v_spill[H][1]),
                      (dout, v_out_prev)]
            w_rA = ([(st, v_heads_end), (st, v_prev_half_st)]
                    + [(dstr[j], v_strips_last[j]) for j in range(4)]
                    + [(dnat[j], v_natc_last[j]) for j in range(4)])

            yinit_dx = [0] * 16
            tpiece_free = [0, 0, 0, 0]
            h2p_free = [0, 0]
            logits_done = [0] * 16
            v_h2f_free = [0, 0]
            v_fh2b_free = [0, 0]

            def emit_gate_tail(H, i, rstd_done):
                hs = i % 2
                ap_done = ln_apply(i, (lambda i=i: sb["y_acc"][:, i * 1024:(i + 1) * 1024]),
                                   (lambda s=hs: sb[f"h2f{s}"][:, :]),
                                   [(ss, rstd_done[i]), (sv, v_h2f_free[hs])], f"lnF{H}_")

                def b_cp(e, s=hs):
                    return e.tensor_copy(sb[f"fh2b{s}"][:, :], sb[f"h2f{s}"][:, :])
                P.op("vector", b_cp, waits=[(dht[hs], v_fh2b_free[hs]),
                                            (dsp[0], v_spill[BPC - 1][0]),
                                            (dsp[1], v_spill[BPC - 1][1])],
                     inc=sv, tag=f"h2b{H}_{i}")
                v_cp = sv.n
                v_h2f_free[hs] = v_cp

                def b_h2t(e, i=i, s=hs):
                    return e.dma_start_transpose(
                        sb["h2T"][:, :].rearrange("p (kk t) -> p kk t", kk=8)[:, :, i * 128:(i + 1) * 128],
                        sb[f"fh2b{s}"][:, :])
                P.op("sync", b_h2t, waits=[(sv, v_cp)] + w_rA, inc=dht[hs], amt=16,
                     tag=f"dmat_h2T_{H}_{i}")
                v_fh2b_free[hs] = dht[hs].n

                for kk in range(8):
                    def b_lg(e, i=i, kk=kk):
                        return e.matmul(psB[:, i * 8: i * 8 + 8],
                                        sb["h2T"][:, kk * 2048 + i * 128: kk * 2048 + i * 128 + 128],
                                        sb["gwb"][:, kk * 8: kk * 8 + 8],
                                        start=(kk == 0), stop=(kk == 7))
                    P.op("tensor", b_lg, waits=[(dht[hs], v_fh2b_free[hs]),
                                                (sv, v_gwb), (sv, v_Gevac_last),
                                                (sv, yslot_free[0]), (sv, yslot_free[1])],
                         inc=st, tag=f"lg{H}_{i}_{kk}")
                    if kk == 7:
                        logits_done[i] = st.n

            for g4 in range(4):
                tiles = [4 * g4 + k for k in range(4)]
                rstd_done = {}
                gens = []
                for i in tiles:
                    r0 = H * S + i * 128

                    def b_ldxp(e, r0=r0, i=i):
                        return e.dma_start(sb["y_acc"][:, i * 1024:(i + 1) * 1024], xp[r0:r0 + 128, :])
                    P.op("sync", b_ldxp,
                         waits=w_yacc + [(dy[i % 8], dy[i % 8].n)],
                         inc=dy[i % 8], amt=16, tag=f"dma_xp_h{H}_{i}")
                    yinit_dx[i] = dy[i % 8].n
                    gens.append(ln_chain(i, (lambda i=i: sb["y_acc"][:, i * 1024:(i + 1) * 1024]),
                                         "fsq",
                                         [(dy[i % 8], yinit_dx[i]),
                                          (dsp[0], v_spill[BPC - 1][0]),
                                          (dsp[1], v_spill[BPC - 1][1]),
                                          (dht[0], v_ht_slot[0]), (dht[1], v_ht_slot[1])],
                                         f"lnF{H}_", rstd_done))
                run_waves(gens)
                for i in tiles:
                    emit_gate_tail(H, i, rstd_done)
            v_h2T = [(dht[0], dht[0].n), (dht[1], dht[1].n)]
            v_logits_all = logits_done[15]

            gst = sb["gstats"]
            lg_ev = [0] * 16
            for i in range(16):
                P.op("vector", lambda e, i=i: e.tensor_copy(
                    sb["lgbuf"][:, i * 8:(i + 1) * 8], psB[:, i * 8:(i + 1) * 8]),
                    waits=[(st, v_logits_all)], inc=sv, tag=f"lgev{H}_{i}")
                lg_ev[i] = sv.n
            g1 = [0] * 16
            for i in range(16):
                P.op("vector", lambda e, i=i: e.reduce_max(
                    gst[:, i:i + 1], sb["lgbuf"][:, i * 8:(i + 1) * 8], AX.X, negate=True),
                    waits=[(sv, lg_ev[i])], inc=sv, tag=f"gm1_{H}_{i}")
                g1[i] = sv.n
            g2 = [0] * 16
            for i in range(16):
                P.op("vector", lambda e, i=i: e.tensor_scalar(
                    sb["gtmp1"][:, i * 8:(i + 1) * 8], sb["lgbuf"][:, i * 8:(i + 1) * 8],
                    gst[:, i:i + 1], 0.0, OP.add, OP.is_ge),
                    waits=[(sv, g1[i])], inc=sv, tag=f"gge_{H}_{i}")
                g2[i] = sv.n
            g3 = [0] * 16
            for i in range(16):
                P.op("vector", lambda e, i=i: e.scalar_tensor_tensor(
                    sb["gtmp2"][:, i * 8:(i + 1) * 8], sb["gtmp1"][:, i * 8:(i + 1) * 8],
                    -1e30, sb["lgbuf"][:, i * 8:(i + 1) * 8], OP.mult, OP.add),
                    waits=[(sv, g2[i])], inc=sv, tag=f"glg2_{H}_{i}")
                g3[i] = sv.n
            g4 = [0] * 16
            for i in range(16):
                P.op("vector", lambda e, i=i: e.reduce_max(
                    gst[:, 16 + i:17 + i], sb["gtmp2"][:, i * 8:(i + 1) * 8], AX.X, negate=True),
                    waits=[(sv, g3[i])], inc=sv, tag=f"gm2_{H}_{i}")
                g4[i] = sv.n
            g5 = [0] * 16
            for i in range(16):
                P.op("vector", lambda e, i=i: e.tensor_scalar(
                    sb["gtmp1"][:, i * 8:(i + 1) * 8], sb["lgbuf"][:, i * 8:(i + 1) * 8],
                    gst[:, 16 + i:17 + i], 0.0, OP.add, OP.is_ge),
                    waits=[(sv, g4[i])], inc=sv, tag=f"gmask_{H}_{i}")
                g5[i] = sv.n
            g6 = [0] * 16
            for i in range(16):
                P.op("scalar", lambda e, i=i: e.activation(
                    sb["getmp"][:, i * 8:(i + 1) * 8], sb["lgbuf"][:, i * 8:(i + 1) * 8],
                    AF.Exp, bias=gst[:, i:i + 1], scale=1.0),
                    waits=[(sv, g1[i])], inc=ss, tag=f"gexp_{H}_{i}")
                g6[i] = ss.n
            g7 = [0] * 16
            for i in range(16):
                P.op("vector", lambda e, i=i: e.tensor_mul(
                    sb["gtmp2"][:, i * 8:(i + 1) * 8], sb["getmp"][:, i * 8:(i + 1) * 8],
                    sb["gtmp1"][:, i * 8:(i + 1) * 8]),
                    waits=[(sv, g5[i]), (ss, g6[i])], inc=sv, tag=f"gw_{H}_{i}")
                g7[i] = sv.n
            g8 = [0] * 16
            for i in range(16):
                P.op("vector", lambda e, i=i: e.reduce_sum(
                    gst[:, i:i + 1], sb["gtmp2"][:, i * 8:(i + 1) * 8], AX.X),
                    waits=[(sv, g7[i])], inc=sv, tag=f"gsum_{H}_{i}")
                g8[i] = sv.n
            g9 = [0] * 16
            for i in range(16):
                P.op("vector", lambda e, i=i: e.reciprocal(
                    gst[:, 32 + i:33 + i], gst[:, i:i + 1]),
                    waits=[(sv, g8[i])], inc=sv, tag=f"grec_{H}_{i}")
                g9[i] = sv.n
            g10 = [0] * 16
            for i in range(16):
                P.op("vector", lambda e, i=i: e.tensor_scalar(
                    gst[:, 48 + i:49 + i], gst[:, 32 + i:33 + i], MOE_SCALE, None, OP.mult),
                    waits=[(sv, g9[i])], inc=sv, tag=f"gcsc_{H}_{i}")
                g10[i] = sv.n
            comb_done = [0] * 16
            for i in range(16):
                P.op("vector", lambda e, i=i: e.tensor_scalar(
                    sb["comb"][:, i * 8:(i + 1) * 8], sb["gtmp2"][:, i * 8:(i + 1) * 8],
                    gst[:, 48 + i:49 + i], None, OP.mult),
                    waits=[(sv, g10[i])], inc=sv, tag=f"gcmb_{H}_{i}")
                comb_done[i] = sv.n
            v_comb = comb_done[15]

            for seg in range(nseg):
                if seg < 8:
                    wgsrc, wusrc = wg_t, wu_t
                    wg_base = seg * D * FF
                    wg_row = FF
                    wdsrc, wd_base = wd_t, seg * FF * D
                else:
                    wgsrc, wusrc = swg_t, swu_t
                    wg_base = (seg - 8) * FF
                    wg_row = 2 * FF
                    wdsrc, wd_base = swd_t, (seg - 8) * FF * D

                wg_ready = {}
                wu_ready = {}

                def emit_wchunk(fk):
                    for (wsrc, ring, freetrk, rdy, sems) in (
                            (wgsrc, "wgr", wgr_free, wg_ready, dwg),
                            (wusrc, "wur", wur_free, wu_ready, dwu)):
                        def b_wg(e, wsrc=wsrc, ring=ring, fk=fk, wg_base=wg_base, wg_row=wg_row):
                            return e.dma_start(
                                sb[f"{ring}{fk % 2}"][:, :].rearrange("p (j f) -> p j f", j=8),
                                bass.AP(wsrc, wg_base + fk * 128,
                                        [[wg_row, 128], [128 * wg_row, 8], [1, 128]]))
                        sem = sems[fk % 2]
                        P.op("gpsimd", b_wg,
                             waits=[(st, freetrk[fk % 2]), (sv, v_Gevac_last)],
                             inc=sem, amt=16, tag=f"dma_w_{H}_{seg}_{fk}_{ring}")
                        rdy[fk] = (sem, sem.n)

                emit_wchunk(0)
                emit_wchunk(1)

                def b_wd(e, wdsrc=wdsrc, wd_base=wd_base):
                    return e.dma_start(
                        sb["wd"][:, :].rearrange("p (fk d) -> p fk d", fk=11),
                        bass.AP(wdsrc, wd_base, [[D, 128], [128 * D, 11], [1, D]]))
                P.op("gpsimd", b_wd,
                     waits=[(st, v_Bend_prev), (st, v_G_end), (sv, v_Gevac_last)],
                     inc=dwd, amt=16, tag=f"dma_wd_{H}_{seg}")
                v_wd = dwd.n

                m_ev_last = 0
                for fk in range(FK):
                    for piece in range(2):
                        sl = (fk * 2 + piece) % 2
                        ps = psA if sl == 0 else psB
                        first_w = [wg_ready[fk]] + list(v_h2T) + [
                                   (sv, aslot_free[sl]), (st, v_logits_all),
                                   (ss, tpiece_free[0]), (ss, tpiece_free[1]),
                                   (ss, tpiece_free[2]), (ss, tpiece_free[3]),
                                   (sv, lg_ev[15]), (sv, yslot_free[sl])]
                        for dd in range(8):
                            for tq in range(2):
                                def b_gA(e, ps=ps, tq=tq, fk=fk, dd=dd, piece=piece):
                                    return e.matmul(
                                        ps[:, tq * 512: tq * 512 + 512],
                                        sb[f"wgr{fk % 2}"][:, dd * 128: dd * 128 + 128],
                                        sb["h2T"][:, dd * 2048 + piece * 1024 + tq * 512:
                                                  dd * 2048 + piece * 1024 + tq * 512 + 512],
                                        start=(dd == 0), stop=(dd == 7))
                                P.op("tensor", b_gA,
                                     waits=first_w if (dd == 0 and tq == 0) else (),
                                     inc=st if (dd == 7 and tq == 1) else None,
                                     tag=f"Ag{H}_{seg}_{fk}_{piece}")
                        v_gate = st.n
                        wgr_free[fk % 2] = v_gate
                        for dd in range(8):
                            for tq in range(2):
                                def b_uA(e, ps=ps, tq=tq, fk=fk, dd=dd, piece=piece):
                                    return e.matmul(
                                        ps[:, 1024 + tq * 512: 1024 + tq * 512 + 512],
                                        sb[f"wur{fk % 2}"][:, dd * 128: dd * 128 + 128],
                                        sb["h2T"][:, dd * 2048 + piece * 1024 + tq * 512:
                                                  dd * 2048 + piece * 1024 + tq * 512 + 512],
                                        start=(dd == 0), stop=(dd == 7))
                                P.op("tensor", b_uA,
                                     waits=[wu_ready[fk]] if (dd == 0 and tq == 0) else (),
                                     inc=st if (dd == 7 and tq == 1) else None,
                                     tag=f"Au{H}_{seg}_{fk}_{piece}")
                        v_up = st.n
                        wur_free[fk % 2] = v_up

                        def b_si(e, ps=ps, sl=sl):
                            return e.activation(sb[f"silu{sl}"][:, :], ps[:, 0:1024], AF.Silu)
                        P.op("scalar", b_si, waits=[(st, v_gate), (sv, v_Gevac_last),
                                                    (sv, aslot_free[sl])],
                             inc=ss, tag=f"si{H}_{seg}_{fk}_{piece}")
                        v_si = ss.n

                        def b_m(e, ps=ps, sl=sl, fk=fk, piece=piece):
                            return e.tensor_mul(
                                sb["m_T"][:, fk * 2048 + piece * 1024: fk * 2048 + piece * 1024 + 1024],
                                sb[f"silu{sl}"][:, :], ps[:, 1024:2048])
                        P.op("vector", b_m, waits=[(ss, v_si), (st, v_up), (st, v_G_end),
                                                   (st, v_Bend_prev)],
                             inc=sv, tag=f"mev{H}_{seg}_{fk}_{piece}")
                        aslot_free[sl] = sv.n
                        m_ev_last = sv.n
                    if fk + 2 < FK:
                        emit_wchunk(fk + 2)

                for ts_ in range(16):
                    sl = ts_ % 2
                    ps = psA if sl == 0 else psB
                    first_w = [(sv, m_ev_last), (dwd, v_wd), (sv, yslot_free[sl])]
                    for fk in range(FK):
                        for dq in range(2):
                            def b_B(e, ps=ps, dq=dq, fk=fk, ts_=ts_):
                                return e.matmul(
                                    ps[:, dq * 512: dq * 512 + 512],
                                    sb["m_T"][:, fk * 2048 + ts_ * 128: fk * 2048 + ts_ * 128 + 128],
                                    sb["wd"][:, fk * 1024 + dq * 512: fk * 1024 + dq * 512 + 512],
                                    start=(fk == 0), stop=(fk == FK - 1))
                            P.op("tensor", b_B,
                                 waits=first_w if (fk == 0 and dq == 0) else (),
                                 inc=st if (fk == FK - 1 and dq == 1) else None,
                                 tag=f"B{H}_{seg}_{ts_}")
                    v_B = st.n
                    v_Bend_prev = v_B

                    if seg < 8:
                        def b_y(e, ps=ps, ts_=ts_, seg=seg):
                            return e.scalar_tensor_tensor(
                                sb["y_acc"][:, ts_ * 1024:(ts_ + 1) * 1024],
                                ps[:, 0:1024],
                                sb["comb"][:, ts_ * 8 + seg: ts_ * 8 + seg + 1],
                                sb["y_acc"][:, ts_ * 1024:(ts_ + 1) * 1024],
                                OP.mult, OP.add)
                    else:
                        def b_y(e, ps=ps, ts_=ts_):
                            return e.tensor_add(
                                sb["y_acc"][:, ts_ * 1024:(ts_ + 1) * 1024],
                                sb["y_acc"][:, ts_ * 1024:(ts_ + 1) * 1024],
                                ps[:, 0:1024])
                    P.op("vector", b_y,
                         waits=[(st, v_B), (sv, v_comb), (dy[ts_ % 8], yinit_dx[ts_])],
                         inc=sv, tag=f"yev{H}_{seg}_{ts_}")
                    yslot_free[sl] = sv.n
                    if seg == nseg - 1:
                        v_yev = sv.n

                        def b_out(e, H=H, ts_=ts_):
                            r0 = H * S + ts_ * 128
                            return e.dma_start(out_t.ap()[r0:r0 + 128, :],
                                               sb["y_acc"][:, ts_ * 1024:(ts_ + 1) * 1024])
                        P.op("sync", b_out, waits=[(sv, v_yev)], inc=dout, amt=16,
                             tag=f"out{H}_{ts_}")
            v_y_all = sv.n
            v_prev_half_st = st.n
            v_out_prev = dout.n

    if dump is not None:
        if dump in ("stats", "gstats", "comb", "lgbuf"):
            w = sb[dump].ap().shape[1]
            def b_dump(e, dump=dump, w=w):
                return e.dma_start(out_t.ap()[0:128, 0:w], sb[dump][:, 0:w])
        else:
            def b_dump(e, dump=dump):
                return e.dma_start(
                    bass.AP(out_t, 0, [[D, 128], [128 * D, 16], [1, D]]),
                    sb[dump][:, :].rearrange("p (t d) -> p t d", t=16))
        P.op("gpsimd", b_dump,
             waits=[(ss, ss.n), (sv, sv.n), (st, st.n)]
                   + [(s, s.n) for s in dht + dstr + dnat + dx + dy]
                   + [(dsp[0], dsp[0].n), (dsp[1], dsp[1].n)],
             inc=dout, amt=16, tag="dump")

    P.op("sync", None, waits=[(dsp[0], dsp[0].n), (dsp[1], dsp[1].n),
                              (dout, dout.n)], tag="final_wait")

    P.check()

    import contextlib
    with contextlib.ExitStack() as ctx:
        for s in P.sems:
            s.h = ctx.enter_context(nc.semaphore(s.name))
        with nc.Block() as block:
            for eng_name in Plan.ENGS:
                ops = P.ops[eng_name]

                def body(e, ops=ops):
                    for f in ops:
                        f(e)
                getattr(block, eng_name)(body)
    probe = nc.alloc_sbuf_tensor("_guard_probe", [128, 4], mybir.dt.uint8)
    paddr = nc.lookup_mloc(probe).addr
    assert paddr + 4 <= SB0, f"const pool collided with kernel map: {paddr} >= {SB0}"
    return nc


_NC_CACHE = {}


def _get_nc(key="full", **kw):
    if key not in _NC_CACHE:
        _NC_CACHE[key] = _build_program(**kw)
    return _NC_CACHE[key]


def kernel(L0_attn_qz, L0_attn_ternary_factor_u, L0_attn_ternary_factor_v,
           L0_ffn_gate_w, L0_ffn_w_gate, L0_ffn_w_up, L0_ffn_w_down,
           L0_ffn_sw_gate, L0_ffn_sw_up, L0_ffn_sw_down):
    qz = np.ascontiguousarray(np.asarray(L0_attn_qz, dtype=np.float32))
    u = np.ascontiguousarray(np.asarray(L0_attn_ternary_factor_u, dtype=np.float32))
    v = np.ascontiguousarray(np.asarray(L0_attn_ternary_factor_v, dtype=np.float32))
    gw = np.ascontiguousarray(np.asarray(L0_ffn_gate_w, dtype=np.float32))
    wg = np.ascontiguousarray(np.asarray(L0_ffn_w_gate, dtype=np.float32))
    wu = np.ascontiguousarray(np.asarray(L0_ffn_w_up, dtype=np.float32))
    wd = np.ascontiguousarray(np.asarray(L0_ffn_w_down, dtype=np.float32))
    swg = np.ascontiguousarray(np.asarray(L0_ffn_sw_gate, dtype=np.float32))
    swu = np.ascontiguousarray(np.asarray(L0_ffn_sw_up, dtype=np.float32))
    swd = np.ascontiguousarray(np.asarray(L0_ffn_sw_down, dtype=np.float32))

    nc = _get_nc()
    in_maps = []
    for i in range(NCORE):
        xs = np.ascontiguousarray(qz[BPC * i: BPC * (i + 1)].reshape(T, D))
        in_maps.append(dict(x=xs, u=u, v=v, gw=gw, wg=wg, wu=wu, wd=wd,
                            swg=swg, swu=swu, swd=swd))
    res = run_bass_kernel_spmd(nc, in_maps, core_ids=list(range(NCORE)))
    shards = [res.results[i]["out"].reshape(BPC, S, D) for i in range(NCORE)]
    return np.concatenate(shards, axis=0)


# version 12
# speedup vs baseline: 1.0589x; 1.0589x over previous
import numpy as np
import concourse.bass as bass
import concourse.mybir as mybir
from concourse.bass_utils import run_bass_kernel_spmd

F32 = mybir.dt.float32
BF16 = mybir.dt.bfloat16
AF = mybir.ActivationFunctionType
OP = mybir.AluOpType
AX = mybir.AxisListType

NCORE = 8
BPC = 2
S = 2048
D = 1024
C = 16
T = BPC * S
E = 8
FF = 1408
NSEG = 10
FK = FF // 128
SCALE = 1.0 / 32.0
MOE_SCALE = 2.5
EPS = 1e-5

SBUF_TOP = 229344
MAP_BYTES = 11264 + 196608
SB0 = SBUF_TOP - MAP_BYTES
PB = SB0 + 11264

PERM = dict(
    id32=(0 + SB0, [128, 128], F32),
    iota_r=(512 + SB0, [128, 128], F32),
    iota_c=(1024 + SB0, [128, 1], F32),
    gw_nat=(1056 + SB0, [8, 1024], F32),
    gwT=(5152 + SB0, [128, 64], F32),
    stats=(5408 + SB0, [128, 128], F32),
    gstats=(5920 + SB0, [128, 64], F32),
    lgbuf=(6176 + SB0, [128, 128], F32),
    comb=(6688 + SB0, [128, 128], F32),
    gtmp1=(7200 + SB0, [128, 128], F32),
    gtmp2=(7712 + SB0, [128, 128], F32),
    getmp=(8224 + SB0, [128, 128], F32),
    h2p32=(8736 + SB0, [128, 256], F32),
    zbuf=(9760 + SB0, [1, 640], BF16),
)

ATTN = dict(
    A1=(PB + 0, [128, 16384], BF16),
    A2=(PB + 32768, [128, 16384], BF16),
    h_T=(PB + 65536, [128, 16384], BF16),
    p_row0=(PB + 65536, [128, 2048], BF16),
    p_row1=(PB + 69632, [128, 2048], BF16),
    ptsG0=(PB + 73728, [128, 4096], BF16),
    ptsG1=(PB + 81920, [128, 4096], BF16),
    unatc0=(PB + 90112, [128, 1024], BF16),
    unatc1=(PB + 92160, [128, 1024], BF16),
    vnatc0=(PB + 94208, [128, 1024], BF16),
    vnatc1=(PB + 96256, [128, 1024], BF16),
    UT=(PB + 98304, [128, 8192], BF16),
    VT=(PB + 114688, [128, 8192], BF16),
    U_nat=(PB + 131072, [128, 8192], BF16),
    V_nat=(PB + 147456, [128, 8192], BF16),
    x_in0=(PB + 163840, [128, 1024], F32),
    x_in1=(PB + 167936, [128, 1024], F32),
    x_in2=(PB + 172032, [128, 1024], F32),
    x_in3=(PB + 176128, [128, 1024], F32),
    sq=(PB + 180224, [128, 1024], F32),
    h_t0=(PB + 184320, [128, 1024], BF16),
    h_t1=(PB + 186368, [128, 1024], BF16),
    xp_out0=(PB + 188416, [128, 1024], F32),
    xp_out1=(PB + 192512, [128, 1024], F32),
)

FFN = dict(
    y_acc=(PB + 0, [128, 16384], F32),
    h2T=(PB + 65536, [128, 16384], BF16),
    m_T=(PB + 98304, [128, 22528], BF16),
    wd=(PB + 143360, [128, 11264], BF16),
    wgr0=(PB + 165888, [128, 1024], BF16),
    wgr1=(PB + 167936, [128, 1024], BF16),
    wur0=(PB + 169984, [128, 1024], BF16),
    wur1=(PB + 172032, [128, 1024], BF16),
    silu0=(PB + 174080, [128, 1024], BF16),
    silu1=(PB + 176128, [128, 1024], BF16),
    h2f0=(PB + 178176, [128, 1024], F32),
    h2f1=(PB + 182272, [128, 1024], F32),
    fsq=(PB + 186368, [128, 1024], F32),
    fh2b0=(PB + 190464, [128, 1024], BF16),
    fh2b1=(PB + 192512, [128, 1024], BF16),
)


class Sem:
    __slots__ = ("name", "n", "h")

    def __init__(self, name):
        self.name, self.n, self.h = name, 0, None


class Plan:
    ENGS = ("tensor", "vector", "scalar", "gpsimd", "sync")

    def __init__(self):
        self.ops = {e: [] for e in self.ENGS}
        self.sems = []
        self._waited = {e: {} for e in self.ENGS}

    def sem(self, name):
        s = Sem(name)
        self.sems.append(s)
        return s

    def op(self, eng, build, waits=(), inc=None, amt=1, tag=""):
        wl = []
        seen = self._waited[eng]
        for (s, v) in waits:
            if v is None or v <= 0:
                continue
            if seen.get(s.name, -1) >= v:
                continue
            seen[s.name] = v
            wl.append((s, v))
        if inc is not None:
            inc.n += amt
        post = inc.n if inc else None

        def f(e, _b=build, _w=wl, _i=inc, _a=amt):
            for s, v in _w:
                e.wait_ge(s.h, v)
            if _b is not None:
                ins = _b(e)
                if _i is not None:
                    ins.then_inc(_i.h, _a)

        f.tag = tag
        f.waits = wl
        f.inc = (inc, amt) if inc else None
        self.ops[eng].append(f)
        return post

    def check(self):
        cnt = {s.name: 0 for s in self.sems}
        idx = {e: 0 for e in self.ENGS}
        progress = True
        while progress:
            progress = False
            for e in self.ENGS:
                while idx[e] < len(self.ops[e]):
                    f = self.ops[e][idx[e]]
                    if all(cnt[s.name] >= v for s, v in f.waits):
                        if f.inc:
                            cnt[f.inc[0].name] += f.inc[1]
                        idx[e] += 1
                        progress = True
                    else:
                        break
        stuck = {e: len(self.ops[e]) - idx[e] for e in self.ENGS if idx[e] < len(self.ops[e])}
        if stuck:
            msg = []
            for e in stuck:
                f = self.ops[e][idx[e]]
                blocked = [(s.name, v, cnt[s.name]) for s, v in f.waits if cnt[s.name] < v]
                msg.append(f"{e}[{idx[e]}] tag={f.tag} blocked on {blocked}")
            raise RuntimeError("plan deadlock:\n" + "\n".join(msg))


def _build_program(do_attn=True, do_ffn=True, heads=C, nseg=NSEG, debug_xp_out=False,
                   bmax=BPC, do_proj=True, do_g=True, dump=None, ffn_halves=BPC):
    nc = bass.Bass("TRN2", target_bir_lowering=False)

    x_t = nc.dram_tensor("x", [T, D], F32, kind="ExternalInput")
    u_t = nc.dram_tensor("u", [D, D], F32, kind="ExternalInput")
    v_t = nc.dram_tensor("v", [D, D], F32, kind="ExternalInput")
    gw_t = nc.dram_tensor("gw", [E, D], F32, kind="ExternalInput")
    wg_t = nc.dram_tensor("wg", [E, D, FF], F32, kind="ExternalInput")
    wu_t = nc.dram_tensor("wu", [E, D, FF], F32, kind="ExternalInput")
    wd_t = nc.dram_tensor("wd", [E, FF, D], F32, kind="ExternalInput")
    swg_t = nc.dram_tensor("swg", [D, 2 * FF], F32, kind="ExternalInput")
    swu_t = nc.dram_tensor("swu", [D, 2 * FF], F32, kind="ExternalInput")
    swd_t = nc.dram_tensor("swd", [2 * FF, D], F32, kind="ExternalInput")
    out_t = nc.dram_tensor("out", [T, D], F32, kind="ExternalOutput")
    xp_t = nc.dram_tensor("xp", [T, D], F32)

    sb = {}
    for table in (PERM, ATTN, FFN):
        for name, (off, shape, dt) in table.items():
            sb[name] = nc.alloc_sbuf_tensor_at(name, shape, dt, offset=off)

    psA = nc.alloc_psum_tensor("psA", [128, 2048], F32)
    psB = nc.alloc_psum_tensor("psB", [128, 2048], F32)

    P = Plan()
    dx = [P.sem(f"dx{j}") for j in range(4)]
    dy = [P.sem(f"dy{j}") for j in range(8)]
    dwg = [P.sem(f"dwg{j}") for j in range(2)]
    dwu = [P.sem(f"dwu{j}") for j in range(2)]
    dwd = P.sem("dwd")
    duv = P.sem("duv")
    dgw = P.sem("dgw")
    duvt = P.sem("duvt")
    dht = [P.sem(f"dht{j}") for j in range(2)]
    dnat = [P.sem(f"dnat{j}") for j in range(4)]
    dstr = [P.sem(f"dstr{j}") for j in range(4)]
    dsp = [P.sem(f"dsp{j}") for j in range(2)]
    dout = P.sem("dout")
    st = P.sem("st")
    sv = P.sem("sv")
    ss = P.sem("ss")
    sg = P.sem("sg")

    x = x_t.ap()
    xp = xp_t.ap()

    P.op("gpsimd", lambda e: e.iota(sb["iota_r"][:, :], [[1, 128]], channel_multiplier=0,
                                    allow_small_or_imprecise_dtypes=True), inc=sg, tag="iota_r")
    P.op("gpsimd", lambda e: e.iota(sb["iota_c"][:, :], [[0, 1]], channel_multiplier=1,
                                    allow_small_or_imprecise_dtypes=True), inc=sg, tag="iota_c")
    v_iota = sg.n
    P.op("vector", lambda e: e.tensor_scalar(sb["id32"][:, :], sb["iota_r"][:, :],
                                             sb["iota_c"][:, 0:1], None, OP.is_equal),
         waits=[(sg, v_iota)], inc=sv, tag="id32")
    v_id32 = sv.n
    P.op("vector", lambda e: e.memset(sb["zbuf"][:, :], 0.0), inc=sv, tag="zbuf")
    v_zbuf = sv.n

    P.op("gpsimd", lambda e: e.dma_start(
        sb["U_nat"][:, :].rearrange("p (c d) -> p c d", c=8),
        bass.AP(u_t, 0, [[D, 128], [128 * D, 8], [1, D]])), inc=duv, amt=16, tag="dma_Unat")
    P.op("gpsimd", lambda e: e.dma_start(
        sb["V_nat"][:, :].rearrange("p (c d) -> p c d", c=8),
        bass.AP(v_t, 0, [[D, 128], [128 * D, 8], [1, D]])), inc=duv, amt=16, tag="dma_Vnat")
    v_uvnat = duv.n

    P.op("sync", lambda e: e.dma_start(sb["gw_nat"][:, :], gw_t.ap()[:, :]),
         inc=dgw, amt=16, tag="dma_gw")
    v_gwload = dgw.n

    for mat, src, dst in (("u", "U_nat", "UT"), ("v", "V_nat", "VT")):
        for j in range(8):
            def b_t(e, src=src, dst=dst, j=j):
                return e.dma_start_transpose(
                    sb[dst][:, :].rearrange("p (kk c) -> p kk c", kk=8)[:, :, j * 128:(j + 1) * 128],
                    sb[src][:, j * 1024:(j + 1) * 1024])
            P.op("sync", b_t, waits=[(duv, v_uvnat)], inc=duvt, amt=16, tag=f"dmat_{dst}{j}")
    v_utvt = duvt.n

    for kk in range(8):
        def b_gwt(e, kk=kk):
            return e.transpose(psB[:, kk * 8:(kk + 1) * 8], sb["gw_nat"][0:8, kk * 128:(kk + 1) * 128],
                               sb["id32"][0:8, 0:8])
        P.op("tensor", b_gwt, waits=[(dgw, v_gwload), (sv, v_id32)], inc=st, tag=f"tr_gw{kk}")
    v_gwtr = st.n
    P.op("scalar", lambda e: e.activation(sb["gwT"][:, 0:64], psB[:, 0:64], AF.Copy),
         waits=[(st, v_gwtr)], inc=ss, tag="ev_gwT")
    v_gwT = ss.n

    for wz in range(192):
        P.op("tensor", lambda e: e.matmul(psA[:, 0:128], sb["id32"][:, :],
                                          sb["id32"][:, :], start=True, stop=True),
             waits=[(sv, v_id32)], tag=f"warm{wz}")

    def ln_chain(i, src_ap_fn, sqbuf, waits_first, tagp, rstd_done):
        s0 = (i % 16) * 8
        stats = sb["stats"]
        P.op("vector", lambda e: e.reduce_sum(stats[:, s0:s0 + 1], src_ap_fn(), AX.X, negate=True),
             waits=list(waits_first), inc=sv, tag=f"{tagp}{i}_sum")
        v1 = sv.n
        yield
        P.op("vector", lambda e: e.scalar_tensor_tensor(sb[sqbuf][:, :], src_ap_fn(), 1.0,
                                                        src_ap_fn(), OP.mult, OP.mult,
                                                        accum_out=stats[:, s0 + 1:s0 + 2]),
             inc=sv, tag=f"{tagp}{i}_sq")
        v2 = sv.n
        yield
        P.op("vector", lambda e: e.tensor_scalar(stats[:, s0 + 2:s0 + 3], stats[:, s0:s0 + 1],
                                                 1.0 / D, None, OP.mult),
             waits=[(sv, v1)], inc=sv, tag=f"{tagp}{i}_negm")
        v3 = sv.n
        yield
        P.op("vector", lambda e: e.tensor_scalar(stats[:, s0 + 3:s0 + 4], stats[:, s0 + 1:s0 + 2],
                                                 1.0 / D, EPS, OP.mult, OP.add),
             waits=[(sv, v2)], inc=sv, tag=f"{tagp}{i}_q")
        v4 = sv.n
        yield
        P.op("vector", lambda e: e.tensor_scalar(stats[:, s0 + 4:s0 + 5], stats[:, s0 + 2:s0 + 3],
                                                 stats[:, s0 + 2:s0 + 3], None, OP.mult),
             waits=[(sv, v3)], inc=sv, tag=f"{tagp}{i}_m2")
        v5 = sv.n
        yield
        P.op("vector", lambda e: e.tensor_sub(stats[:, s0 + 5:s0 + 6], stats[:, s0 + 3:s0 + 4],
                                              stats[:, s0 + 4:s0 + 5]),
             waits=[(sv, v5)], inc=sv, tag=f"{tagp}{i}_ve")
        v6 = sv.n
        yield
        P.op("vector", lambda e: e.reciprocal(stats[:, s0 + 6:s0 + 7], stats[:, s0 + 5:s0 + 6]),
             waits=[(sv, v6)], inc=sv, tag=f"{tagp}{i}_ri")
        v7 = sv.n
        yield
        P.op("scalar", lambda e: e.activation(stats[:, s0 + 7:s0 + 8], stats[:, s0 + 6:s0 + 7], AF.Sqrt),
             waits=[(sv, v7)], inc=ss, tag=f"{tagp}{i}_rstd")
        rstd_done[i] = ss.n

    def run_waves(gens):
        alive = list(gens)
        while alive:
            nxt = []
            for g in alive:
                try:
                    next(g)
                    nxt.append(g)
                except StopIteration:
                    pass
            alive = nxt

    def ln_apply(i, src_ap_fn, out_ap_fn, apply_waits, tagp):
        s0 = (i % 16) * 8
        stats = sb["stats"]
        P.op("vector", lambda e: e.tensor_scalar(
            out_ap_fn(), src_ap_fn(), stats[:, s0 + 2:s0 + 3], stats[:, s0 + 7:s0 + 8],
            OP.add, OP.mult), waits=list(apply_waits), inc=sv, tag=f"{tagp}{i}_app")
        return sv.n

    xin_free = [0, 0, 0, 0]
    xin_load = [0, 0, 0, 0]
    xpout_free = [0, 0]
    v_G_end = 0
    v_Gevac_last = 0
    v_spill = [[0, 0], [0, 0]]
    v_heads_end = 0
    v_strips_last = [0, 0, 0, 0]
    v_natc_last = [0, 0, 0, 0]
    v_ht_slot = [0, 0]
    v_sig_all = 0

    pieces = [(psA, 0), (psA, 1024), (psB, 0), (psB, 1024)]

    hT_done = {}

    def emit_attn_ln_tile_tail(b, i, rstd_done, prev_heads_end, prev_strips, prev_natc):
        slot, hs = i % 4, i % 2
        ap_done = ln_apply(i, (lambda s=slot: sb[f"x_in{s}"][:, :]),
                           (lambda s=hs: sb[f"h_t{s}"][:, :]),
                           [(ss, rstd_done[i]), (dht[hs], v_ht_slot[hs])], f"lnA{b}_")
        xin_free[slot] = ap_done

        def b_ttr(e, i=i, s=hs):
            return e.dma_start_transpose(
                sb["h_T"][:, :].rearrange("p (kk t) -> p kk t", kk=8)[:, :, i * 128:(i + 1) * 128],
                sb[f"h_t{s}"][:, :])
        P.op("sync", b_ttr, waits=[(sv, ap_done), (st, prev_heads_end)]
             + [(dstr[j], prev_strips[j]) for j in range(4)]
             + [(dnat[j], prev_natc[j]) for j in range(4)],
             inc=dht[hs], amt=16, tag=f"dmat_hT_b{b}_{i}")
        v_ht_slot[hs] = dht[hs].n
        hT_done[(b, i)] = dht[hs].n

    def emit_attn_ln_x_load(b, i):
        slot = i % 4
        r0 = b * S + i * 128

        def b_ldx(e, r0=r0, slot=slot):
            return e.dma_start(sb[f"x_in{slot}"][:, :], x[r0:r0 + 128, :])
        P.op("sync", b_ldx, waits=[(sv, xin_free[slot])], inc=dx[slot], amt=16,
             tag=f"dma_x_b{b}_{i}")
        xin_load[slot] = dx[slot].n

    def emit_attn_ln_group(b, g4, prev_heads_end, prev_strips, prev_natc):
        tiles = [4 * g4 + k for k in range(4)]
        rstd_done = {}
        gens = []
        for i in tiles:
            emit_attn_ln_x_load(b, i)
            gens.append(ln_chain(i, (lambda s=i % 4: sb[f"x_in{s}"][:, :]), "sq",
                                 [(dx[i % 4], xin_load[i % 4])], f"lnA{b}_", rstd_done))
        run_waves(gens)
        for k, i in enumerate(tiles):
            emit_attn_ln_tile_tail(b, i, rstd_done, prev_heads_end, prev_strips, prev_natc)

    def emit_attn_ln_pair(b, i0, prev_heads_end, prev_strips, prev_natc):
        rstd_done = {}
        gens = []
        for i in (i0, i0 + 1):
            emit_attn_ln_x_load(b, i)
            gens.append(ln_chain(i, (lambda s=i % 4: sb[f"x_in{s}"][:, :]), "sq",
                                 [(dx[i % 4], xin_load[i % 4])], f"lnA{b}_", rstd_done))
        run_waves(gens)
        for i in (i0, i0 + 1):
            emit_attn_ln_tile_tail(b, i, rstd_done, prev_heads_end, prev_strips, prev_natc)

    if do_attn:
        for g4 in range(4):
            emit_attn_ln_group(0, g4, 0, [0, 0, 0, 0], [0, 0, 0, 0])
        for b in range(bmax):
            qpieces = [(psA, 0), (psA, 512), (psA, 1024), (psA, 1536),
                       (psB, 0), (psB, 512), (psB, 1024), (psB, 1536)]
            qpiece_free = [0] * 8
            for q in range(4):
                if not do_proj:
                    break
                for im, (wbuf, obuf) in enumerate((("UT", "A1"), ("VT", "A2"))):
                    for d_out in range(8):
                        g = q * 16 + im * 8 + d_out
                        ps, off = qpieces[g % 8]
                        first_waits = [(dht[(q * 4 + 3) % 2], hT_done[(b, q * 4 + 3)]),
                                       (dht[(q * 4 + 2) % 2], hT_done[(b, q * 4 + 2)]),
                                       (duvt, v_utvt),
                                       (ss, qpiece_free[g % 8]),
                                       (sv, v_Gevac_last), (ss, v_gwT)]
                        for kk in range(8):
                            def b_mm(e, ps=ps, off=off, kk=kk, wbuf=wbuf,
                                     d_out=d_out, q=q):
                                return e.matmul(
                                    ps[:, off: off + 512],
                                    sb[wbuf][:, kk * 1024 + d_out * 128: kk * 1024 + d_out * 128 + 128],
                                    sb["h_T"][:, kk * 2048 + q * 512: kk * 2048 + q * 512 + 512],
                                    start=(kk == 0), stop=(kk == 7))
                            P.op("tensor", b_mm,
                                 waits=first_waits if kk == 0 else (),
                                 inc=st if kk == 7 else None,
                                 tag=f"proj{b}_{g}_{kk}")
                        v_g = st.n

                        def b_ev(e, ps=ps, off=off, obuf=obuf, d_out=d_out, q=q):
                            return e.activation(
                                sb[obuf][:, d_out * 2048 + q * 512: d_out * 2048 + q * 512 + 512],
                                ps[:, off:off + 512], AF.Copy, scale=SCALE)
                        P.op("scalar", b_ev, waits=[(st, v_g)], inc=ss, tag=f"projev{b}_{g}")
                        qpiece_free[g % 8] = ss.n
            v_proj_ev = ss.n

            natc_free = [0, 0]
            natc_val = [None, None]
            oevac_prev = 0
            oevac_prev_ss = 0
            prow_free = [0, 0]
            prow_free_dt = [[], []]
            ptsg_free = [0, 0]
            fpair_free = [0, 0]

            def emit_natc(c):
                cc2 = c % 2
                vals = []
                for sidx, (srcb, dstb) in enumerate((("A1", f"unatc{cc2}"), ("A2", f"vnatc{cc2}"))):
                    def b_nat(e, srcb=srcb, dstb=dstb, base=cc2 * 64, chunk=c // 2):
                        return e.dma_start_transpose(
                            sb[dstb][:, :].rearrange("p (j r) -> p j r", j=16),
                            sb[srcb][base:base + 64, chunk * 2048:(chunk + 1) * 2048])
                    sem = dnat[sidx * 2 + cc2]
                    P.op("sync", b_nat, waits=[(ss, v_proj_ev), (st, natc_free[cc2])],
                         inc=sem, amt=16, tag=f"nat{b}_{c}")
                    vals.append((sem, sem.n))
                natc_val[cc2] = vals

            emit_natc(0)
            for c in range(heads):
                cc = c % 2
                base = cc * 64
                chunk = c // 2
                v_natc = natc_val[cc]

                v_zfill = None
                sig_done = {}
                strip_done = {}
                o1_done = {}
                o2_done = {}

                def emit_o2(i):
                    sl = i % 2
                    for tq in range(4):
                        w = [(ss, sig_done[(i, tq // 2)])]
                        if i == 0 and tq == 0:
                            w += [(st, v_zfill)] + list(v_natc)

                        def b_o2(e, sl=sl, tq=tq, i=i, cc=cc):
                            return e.matmul(
                                psB[64:128, tq * 512: tq * 512 + 512],
                                sb[f"unatc{cc}"][:, i * 64: i * 64 + 64],
                                sb[f"p_row{sl}"][:, tq * 512: tq * 512 + 512],
                                start=False, stop=(i == 15), tile_position=(0, 64),
                                skip_group_check=True)
                        P.op("tensor", b_o2, waits=w, inc=st if tq == 3 else None,
                             tag=f"o2_{b}_{c}_{i}_{tq}")
                    o2_done[i] = st.n
                    prow_free[sl] = st.n

                def emit_o1(g):
                    for j in range(16):
                        i1 = 2 * g + 1
                        w = [(dstr[i1 % 4], strip_done[i1]),
                             (dstr[(i1 - 1) % 4], strip_done[i1 - 1])]
                        if g == 0 and j == 0:
                            w += [(st, v_zfill)] + list(v_natc)

                        def b_o1(e, g=g, j=j, cc=cc):
                            return e.matmul(
                                psB[0:64, g * 256: g * 256 + 256],
                                sb[f"vnatc{cc}"][:, j * 64: j * 64 + 64],
                                sb[f"ptsG{g % 2}"][:, j * 256: j * 256 + 256],
                                start=False, stop=(j == 15), tile_position=(0, 0),
                                skip_group_check=True)
                        P.op("tensor", b_o1, waits=w, inc=st if j == 15 else None,
                             tag=f"o1_{b}_{c}_{g}_{j}")
                    o1_done[g] = st.n
                    ptsg_free[g % 2] = st.n

                for i in range(16):
                    sl = i % 2
                    for tq in range(4):
                        h = tq // 2

                        def b_F(e, h=h, tq=tq, base=base, chunk=chunk, i=i):
                            return e.matmul(
                                psA[:, h * 1024 + (tq % 2) * 512: h * 1024 + (tq % 2) * 512 + 512],
                                sb["A1"][base:base + 64, chunk * 2048 + i * 128: chunk * 2048 + i * 128 + 128],
                                sb["A2"][base:base + 64, chunk * 2048 + tq * 512: chunk * 2048 + tq * 512 + 512],
                                start=True, stop=True, tile_position=(base, 0))
                        P.op("tensor", b_F,
                             waits=[(ss, fpair_free[h]), (ss, v_proj_ev)],
                             inc=st if tq % 2 == 1 else None, tag=f"F{b}_{c}_{i}_{tq}")
                        if tq % 2 == 1:
                            v_fpair = st.n
                            def b_sig(e, h=h, sl=sl):
                                return e.activation(
                                    sb[f"p_row{sl}"][:, h * 1024: h * 1024 + 1024],
                                    psA[:, h * 1024: h * 1024 + 1024], AF.Sigmoid)
                            P.op("scalar", b_sig,
                                 waits=[(st, v_fpair), (st, prow_free[sl])]
                                       + prow_free_dt[sl],
                                 inc=ss, tag=f"sig{b}_{c}_{i}_{h}")
                            sig_done[(i, h)] = ss.n
                            fpair_free[h] = ss.n

                    if i == 1:
                        for zq in range(4):
                            def b_z(e, zq=zq):
                                return e.matmul(psB[:, zq * 512: zq * 512 + 512],
                                                sb["zbuf"][0:1, 0:128], sb["zbuf"][0:1, 128:640],
                                                start=True, stop=False, skip_group_check=True)
                            P.op("tensor", b_z, waits=[(sv, oevac_prev), (ss, oevac_prev_ss),
                                                       (sv, v_zbuf), (ss, v_proj_ev)],
                                 inc=st if zq == 3 else None, tag=f"zfill{b}_{c}_{zq}")
                        v_zfill = st.n
                    if i == 8 and c + 1 < heads:
                        emit_natc(c + 1)
                    if i >= 1:
                        emit_o2(i - 1)
                    if i >= 4 and i % 2 == 0:
                        emit_o1((i - 4) // 2)

                    g = i // 2

                    def b_strip(e, sl=sl, g=g, i=i):
                        return e.dma_start_transpose(
                            sb[f"ptsG{g % 2}"][:, :]
                            .rearrange("p (j s) -> p j s", j=16)[:, :, (i % 2) * 128:(i % 2) * 128 + 128],
                            sb[f"p_row{sl}"][:, :])
                    P.op("sync", b_strip, waits=[(ss, sig_done[(i, 1)]),
                                                 (st, ptsg_free[g % 2])],
                         inc=dstr[i % 4], amt=16, tag=f"strip{b}_{c}_{i}")
                    strip_done[i] = dstr[i % 4].n
                    prow_free_dt[sl] = [(dstr[i % 4], dstr[i % 4].n)]


                emit_o2(15)
                emit_o1(6)
                emit_o1(7)
                v_oend = max(o2_done[15], o1_done[7])
                v_heads_end = st.n
                natc_free[cc] = st.n

                def b_ev1(e, base=base, chunk=chunk):
                    return e.tensor_copy(sb["A1"][base:base + 64, chunk * 2048:(chunk + 1) * 2048],
                                         psB[0:64, :])
                P.op("vector", b_ev1, waits=[(st, v_oend)] + list(v_natc), inc=sv,
                     tag=f"oev1_{b}_{c}")

                def b_ev2(e, base=base, chunk=chunk):
                    return e.tensor_copy(sb["A2"][base:base + 64, chunk * 2048:(chunk + 1) * 2048],
                                         psB[64:128, :])
                P.op("vector", b_ev2, inc=sv, tag=f"oev2_{b}_{c}")
                oevac_prev = sv.n
                oevac_prev_ss = 0
            v_oevac_all = sv.n
            v_strips_last = [s.n for s in dstr]
            v_natc_last = [s.n for s in dnat]
            v_sig_all = ss.n

            gpiece_free = [0, 0, 0, 0]
            for t in range(16 if do_g else 0):
                ps, off = pieces[t % 4]
                slot = t % 4
                r0 = b * S + t * 128

                def b_ldx2(e, r0=r0, slot=slot):
                    return e.dma_start(sb[f"x_in{slot}"][:, :], x[r0:r0 + 128, :])
                P.op("sync", b_ldx2, waits=[(sv, xin_free[slot])], inc=dx[slot], amt=16,
                     tag=f"dma_xG_b{b}_{t}")
                xin_load[slot] = dx[slot].n

                first_w = [(sv, v_oevac_all), (ss, oevac_prev_ss), (duv, v_uvnat),
                           (sv, gpiece_free[t % 4]), (ss, v_proj_ev), (ss, v_sig_all)]
                for kk in range(16):
                    src_o = "A1" if kk < 8 else "A2"
                    src_w = "U_nat" if kk < 8 else "V_nat"
                    k8 = kk % 8
                    for dq in range(2):
                        def b_G(e, ps=ps, off=off, dq=dq, src_o=src_o, src_w=src_w,
                                k8=k8, t=t, kk=kk):
                            return e.matmul(
                                ps[:, off + dq * 512: off + dq * 512 + 512],
                                sb[src_o][:, k8 * 2048 + t * 128: k8 * 2048 + t * 128 + 128],
                                sb[src_w][:, k8 * 1024 + dq * 512: k8 * 1024 + dq * 512 + 512],
                                start=(kk == 0), stop=(kk == 15))
                        P.op("tensor", b_G, waits=first_w if (kk == 0 and dq == 0) else (),
                             inc=st if (kk == 15 and dq == 1) else None,
                             tag=f"G{b}_{t}_{kk}_{dq}")
                v_g = st.n
                v_G_end = v_g

                def b_gev(e, ps=ps, off=off, slot=slot, t=t):
                    return e.scalar_tensor_tensor(sb[f"xp_out{t % 2}"][:, :],
                                                  ps[:, off:off + 1024], SCALE,
                                                  sb[f"x_in{slot}"][:, :], OP.mult, OP.add)
                P.op("vector", b_gev, waits=[(st, v_g), (dx[slot], xin_load[slot]),
                                             (dsp[t % 2], xpout_free[t % 2])],
                     inc=sv, tag=f"gev{b}_{t}")
                xin_free[slot] = sv.n
                gpiece_free[t % 4] = sv.n
                v_gev = sv.n
                v_Gevac_last = v_gev

                def b_spill(e, r0=r0, t=t):
                    return e.dma_start(xp[r0:r0 + 128, :], sb[f"xp_out{t % 2}"][:, :])
                P.op("sync", b_spill, waits=[(sv, v_gev)], inc=dsp[t % 2], amt=16,
                     tag=f"spill{b}_{t}")
                xpout_free[t % 2] = dsp[t % 2].n
                v_spill[b] = [dsp[0].n, dsp[1].n]

                if b + 1 < bmax and t % 2 == 1:
                    emit_attn_ln_pair(b + 1, t - 1, v_heads_end, v_strips_last, v_natc_last)

        if debug_xp_out:
            for i in range(min(bmax * 16, T // 128)):
                def b_dbg(e, i=i):
                    return e.dma_start(out_t.ap()[i * 128:(i + 1) * 128, :], xp[i * 128:(i + 1) * 128, :])
                P.op("sync", b_dbg, waits=[(dsp[0], v_spill[bmax - 1][0]),
                                           (dsp[1], v_spill[bmax - 1][1])],
                     inc=dout, amt=16, tag=f"dbgout{i}")

    if do_ffn:
        v_out_prev = 0
        v_prev_half_st = 0
        wgr_free = [0, 0]
        wur_free = [0, 0]
        aslot_free = [0, 0]
        yslot_free = [0, 0]
        v_Bend_prev = 0
        for H in range(ffn_halves):
            w_yacc = [(st, v_G_end), (dsp[0], v_spill[H][0]), (dsp[1], v_spill[H][1]),
                      (dout, v_out_prev)]
            w_rA = ([(st, v_heads_end), (st, v_prev_half_st)]
                    + [(dstr[j], v_strips_last[j]) for j in range(4)]
                    + [(dnat[j], v_natc_last[j]) for j in range(4)])

            yinit_dx = [0] * 16
            tpiece_free = [0, 0, 0, 0]
            h2p_free = [0, 0]
            logits_done = [0] * 16
            v_h2f_free = [0, 0]
            v_fh2b_free = [0, 0]

            def emit_gate_tail(H, i, rstd_done):
                hs = i % 2
                ap_done = ln_apply(i, (lambda i=i: sb["y_acc"][:, i * 1024:(i + 1) * 1024]),
                                   (lambda s=hs: sb[f"h2f{s}"][:, :]),
                                   [(ss, rstd_done[i]), (st, v_h2f_free[hs])], f"lnF{H}_")

                def b_cp(e, s=hs):
                    return e.tensor_copy(sb[f"fh2b{s}"][:, :], sb[f"h2f{s}"][:, :])
                P.op("vector", b_cp, waits=[(dht[hs], v_fh2b_free[hs]),
                                            (dsp[0], v_spill[BPC - 1][0]),
                                            (dsp[1], v_spill[BPC - 1][1])],
                     inc=sv, tag=f"h2b{H}_{i}")
                v_cp = sv.n

                def b_h2t(e, i=i, s=hs):
                    return e.dma_start_transpose(
                        sb["h2T"][:, :].rearrange("p (kk t) -> p kk t", kk=8)[:, :, i * 128:(i + 1) * 128],
                        sb[f"fh2b{s}"][:, :])
                P.op("sync", b_h2t, waits=[(sv, v_cp)] + w_rA, inc=dht[hs], amt=16,
                     tag=f"dmat_h2T_{H}_{i}")
                v_fh2b_free[hs] = dht[hs].n

                for kk in range(8):
                    pidx = (i * 8 + kk) % 4
                    spc = (i * 8 + kk) % 2

                    def b_tr(e, pidx=pidx, kk=kk, s=hs):
                        return e.transpose(psA[:, pidx * 512: pidx * 512 + 128],
                                           sb[f"h2f{s}"][:, kk * 128:(kk + 1) * 128],
                                           sb["id32"][:, :])
                    P.op("tensor", b_tr, waits=[(sv, ap_done), (ss, tpiece_free[pidx]),
                                                (sv, v_Gevac_last),
                                                (sv, yslot_free[0]), (sv, yslot_free[1])],
                         inc=st, tag=f"h2tr{H}_{i}_{kk}")
                    v_tr = st.n
                    if kk == 7:
                        v_h2f_free[hs] = st.n

                    def b_trev(e, pidx=pidx, spc=spc):
                        return e.activation(sb["h2p32"][:, spc * 128: spc * 128 + 128],
                                            psA[:, pidx * 512: pidx * 512 + 128], AF.Copy)
                    P.op("scalar", b_trev, waits=[(st, v_tr), (st, h2p_free[spc]),
                                                  (ss, v_gwT)],
                         inc=ss, tag=f"h2trev{H}_{i}_{kk}")
                    v_te = ss.n
                    tpiece_free[pidx] = ss.n

                    def b_lg(e, spc=spc, i=i, kk=kk):
                        return e.matmul(psB[:, i * 8: i * 8 + 8],
                                        sb["h2p32"][:, spc * 128: spc * 128 + 128],
                                        sb["gwT"][:, kk * 8: kk * 8 + 8],
                                        start=(kk == 0), stop=(kk == 7))
                    P.op("tensor", b_lg, waits=[(ss, v_te), (sv, v_Gevac_last),
                                                (sv, yslot_free[0]), (sv, yslot_free[1])],
                         inc=st, tag=f"lg{H}_{i}_{kk}")
                    h2p_free[spc] = st.n
                    if kk == 7:
                        logits_done[i] = st.n

            for g4 in range(4):
                tiles = [4 * g4 + k for k in range(4)]
                rstd_done = {}
                gens = []
                for i in tiles:
                    r0 = H * S + i * 128

                    def b_ldxp(e, r0=r0, i=i):
                        return e.dma_start(sb["y_acc"][:, i * 1024:(i + 1) * 1024], xp[r0:r0 + 128, :])
                    P.op("sync", b_ldxp,
                         waits=w_yacc + [(dy[i % 8], dy[i % 8].n)],
                         inc=dy[i % 8], amt=16, tag=f"dma_xp_h{H}_{i}")
                    yinit_dx[i] = dy[i % 8].n
                    gens.append(ln_chain(i, (lambda i=i: sb["y_acc"][:, i * 1024:(i + 1) * 1024]),
                                         "fsq",
                                         [(dy[i % 8], yinit_dx[i]),
                                          (dsp[0], v_spill[BPC - 1][0]),
                                          (dsp[1], v_spill[BPC - 1][1]),
                                          (dht[0], v_ht_slot[0]), (dht[1], v_ht_slot[1])],
                                         f"lnF{H}_", rstd_done))
                run_waves(gens)
                for i in tiles:
                    emit_gate_tail(H, i, rstd_done)
            v_h2T = [(dht[0], dht[0].n), (dht[1], dht[1].n)]
            v_logits_all = logits_done[15]

            gst = sb["gstats"]
            lg_ev = [0] * 16
            for i in range(16):
                P.op("vector", lambda e, i=i: e.tensor_copy(
                    sb["lgbuf"][:, i * 8:(i + 1) * 8], psB[:, i * 8:(i + 1) * 8]),
                    waits=[(st, v_logits_all)], inc=sv, tag=f"lgev{H}_{i}")
                lg_ev[i] = sv.n
            g1 = [0] * 16
            for i in range(16):
                P.op("vector", lambda e, i=i: e.reduce_max(
                    gst[:, i:i + 1], sb["lgbuf"][:, i * 8:(i + 1) * 8], AX.X, negate=True),
                    waits=[(sv, lg_ev[i])], inc=sv, tag=f"gm1_{H}_{i}")
                g1[i] = sv.n
            g2 = [0] * 16
            for i in range(16):
                P.op("vector", lambda e, i=i: e.tensor_scalar(
                    sb["gtmp1"][:, i * 8:(i + 1) * 8], sb["lgbuf"][:, i * 8:(i + 1) * 8],
                    gst[:, i:i + 1], 0.0, OP.add, OP.is_ge),
                    waits=[(sv, g1[i])], inc=sv, tag=f"gge_{H}_{i}")
                g2[i] = sv.n
            g3 = [0] * 16
            for i in range(16):
                P.op("vector", lambda e, i=i: e.scalar_tensor_tensor(
                    sb["gtmp2"][:, i * 8:(i + 1) * 8], sb["gtmp1"][:, i * 8:(i + 1) * 8],
                    -1e30, sb["lgbuf"][:, i * 8:(i + 1) * 8], OP.mult, OP.add),
                    waits=[(sv, g2[i])], inc=sv, tag=f"glg2_{H}_{i}")
                g3[i] = sv.n
            g4 = [0] * 16
            for i in range(16):
                P.op("vector", lambda e, i=i: e.reduce_max(
                    gst[:, 16 + i:17 + i], sb["gtmp2"][:, i * 8:(i + 1) * 8], AX.X, negate=True),
                    waits=[(sv, g3[i])], inc=sv, tag=f"gm2_{H}_{i}")
                g4[i] = sv.n
            g5 = [0] * 16
            for i in range(16):
                P.op("vector", lambda e, i=i: e.tensor_scalar(
                    sb["gtmp1"][:, i * 8:(i + 1) * 8], sb["lgbuf"][:, i * 8:(i + 1) * 8],
                    gst[:, 16 + i:17 + i], 0.0, OP.add, OP.is_ge),
                    waits=[(sv, g4[i])], inc=sv, tag=f"gmask_{H}_{i}")
                g5[i] = sv.n
            g6 = [0] * 16
            for i in range(16):
                P.op("scalar", lambda e, i=i: e.activation(
                    sb["getmp"][:, i * 8:(i + 1) * 8], sb["lgbuf"][:, i * 8:(i + 1) * 8],
                    AF.Exp, bias=gst[:, i:i + 1], scale=1.0),
                    waits=[(sv, g1[i])], inc=ss, tag=f"gexp_{H}_{i}")
                g6[i] = ss.n
            g7 = [0] * 16
            for i in range(16):
                P.op("vector", lambda e, i=i: e.tensor_mul(
                    sb["gtmp2"][:, i * 8:(i + 1) * 8], sb["getmp"][:, i * 8:(i + 1) * 8],
                    sb["gtmp1"][:, i * 8:(i + 1) * 8]),
                    waits=[(sv, g5[i]), (ss, g6[i])], inc=sv, tag=f"gw_{H}_{i}")
                g7[i] = sv.n
            g8 = [0] * 16
            for i in range(16):
                P.op("vector", lambda e, i=i: e.reduce_sum(
                    gst[:, i:i + 1], sb["gtmp2"][:, i * 8:(i + 1) * 8], AX.X),
                    waits=[(sv, g7[i])], inc=sv, tag=f"gsum_{H}_{i}")
                g8[i] = sv.n
            g9 = [0] * 16
            for i in range(16):
                P.op("vector", lambda e, i=i: e.reciprocal(
                    gst[:, 32 + i:33 + i], gst[:, i:i + 1]),
                    waits=[(sv, g8[i])], inc=sv, tag=f"grec_{H}_{i}")
                g9[i] = sv.n
            g10 = [0] * 16
            for i in range(16):
                P.op("vector", lambda e, i=i: e.tensor_scalar(
                    gst[:, 48 + i:49 + i], gst[:, 32 + i:33 + i], MOE_SCALE, None, OP.mult),
                    waits=[(sv, g9[i])], inc=sv, tag=f"gcsc_{H}_{i}")
                g10[i] = sv.n
            comb_done = [0] * 16
            for i in range(16):
                P.op("vector", lambda e, i=i: e.tensor_scalar(
                    sb["comb"][:, i * 8:(i + 1) * 8], sb["gtmp2"][:, i * 8:(i + 1) * 8],
                    gst[:, 48 + i:49 + i], None, OP.mult),
                    waits=[(sv, g10[i])], inc=sv, tag=f"gcmb_{H}_{i}")
                comb_done[i] = sv.n
            v_comb = comb_done[15]

            for seg in range(nseg):
                if seg < 8:
                    wgsrc, wusrc = wg_t, wu_t
                    wg_base = seg * D * FF
                    wg_row = FF
                    wdsrc, wd_base = wd_t, seg * FF * D
                else:
                    wgsrc, wusrc = swg_t, swu_t
                    wg_base = (seg - 8) * FF
                    wg_row = 2 * FF
                    wdsrc, wd_base = swd_t, (seg - 8) * FF * D

                wg_ready = {}
                wu_ready = {}

                def emit_wchunk(fk):
                    for (wsrc, ring, freetrk, rdy, sems) in (
                            (wgsrc, "wgr", wgr_free, wg_ready, dwg),
                            (wusrc, "wur", wur_free, wu_ready, dwu)):
                        def b_wg(e, wsrc=wsrc, ring=ring, fk=fk, wg_base=wg_base, wg_row=wg_row):
                            return e.dma_start(
                                sb[f"{ring}{fk % 2}"][:, :].rearrange("p (j f) -> p j f", j=8),
                                bass.AP(wsrc, wg_base + fk * 128,
                                        [[wg_row, 128], [128 * wg_row, 8], [1, 128]]))
                        sem = sems[fk % 2]
                        P.op("gpsimd", b_wg,
                             waits=[(st, freetrk[fk % 2]), (sv, v_Gevac_last)],
                             inc=sem, amt=16, tag=f"dma_w_{H}_{seg}_{fk}_{ring}")
                        rdy[fk] = (sem, sem.n)

                emit_wchunk(0)
                emit_wchunk(1)

                def b_wd(e, wdsrc=wdsrc, wd_base=wd_base):
                    return e.dma_start(
                        sb["wd"][:, :].rearrange("p (fk d) -> p fk d", fk=11),
                        bass.AP(wdsrc, wd_base, [[D, 128], [128 * D, 11], [1, D]]))
                P.op("gpsimd", b_wd,
                     waits=[(st, v_Bend_prev), (st, v_G_end), (sv, v_Gevac_last)],
                     inc=dwd, amt=16, tag=f"dma_wd_{H}_{seg}")
                v_wd = dwd.n

                m_ev_last = 0
                for fk in range(FK):
                    for piece in range(2):
                        sl = (fk * 2 + piece) % 2
                        ps = psA if sl == 0 else psB
                        first_w = [wg_ready[fk]] + list(v_h2T) + [
                                   (sv, aslot_free[sl]), (st, v_logits_all),
                                   (ss, tpiece_free[0]), (ss, tpiece_free[1]),
                                   (ss, tpiece_free[2]), (ss, tpiece_free[3]),
                                   (sv, lg_ev[15]), (sv, yslot_free[sl])]
                        for dd in range(8):
                            for tq in range(2):
                                def b_gA(e, ps=ps, tq=tq, fk=fk, dd=dd, piece=piece):
                                    return e.matmul(
                                        ps[:, tq * 512: tq * 512 + 512],
                                        sb[f"wgr{fk % 2}"][:, dd * 128: dd * 128 + 128],
                                        sb["h2T"][:, dd * 2048 + piece * 1024 + tq * 512:
                                                  dd * 2048 + piece * 1024 + tq * 512 + 512],
                                        start=(dd == 0), stop=(dd == 7))
                                P.op("tensor", b_gA,
                                     waits=first_w if (dd == 0 and tq == 0) else (),
                                     inc=st if (dd == 7 and tq == 1) else None,
                                     tag=f"Ag{H}_{seg}_{fk}_{piece}")
                        v_gate = st.n
                        wgr_free[fk % 2] = v_gate
                        for dd in range(8):
                            for tq in range(2):
                                def b_uA(e, ps=ps, tq=tq, fk=fk, dd=dd, piece=piece):
                                    return e.matmul(
                                        ps[:, 1024 + tq * 512: 1024 + tq * 512 + 512],
                                        sb[f"wur{fk % 2}"][:, dd * 128: dd * 128 + 128],
                                        sb["h2T"][:, dd * 2048 + piece * 1024 + tq * 512:
                                                  dd * 2048 + piece * 1024 + tq * 512 + 512],
                                        start=(dd == 0), stop=(dd == 7))
                                P.op("tensor", b_uA,
                                     waits=[wu_ready[fk]] if (dd == 0 and tq == 0) else (),
                                     inc=st if (dd == 7 and tq == 1) else None,
                                     tag=f"Au{H}_{seg}_{fk}_{piece}")
                        v_up = st.n
                        wur_free[fk % 2] = v_up

                        def b_si(e, ps=ps, sl=sl):
                            return e.activation(sb[f"silu{sl}"][:, :], ps[:, 0:1024], AF.Silu)
                        P.op("scalar", b_si, waits=[(st, v_gate), (sv, v_Gevac_last),
                                                    (sv, aslot_free[sl])],
                             inc=ss, tag=f"si{H}_{seg}_{fk}_{piece}")
                        v_si = ss.n

                        def b_m(e, ps=ps, sl=sl, fk=fk, piece=piece):
                            return e.tensor_mul(
                                sb["m_T"][:, fk * 2048 + piece * 1024: fk * 2048 + piece * 1024 + 1024],
                                sb[f"silu{sl}"][:, :], ps[:, 1024:2048])
                        P.op("vector", b_m, waits=[(ss, v_si), (st, v_up), (st, v_G_end),
                                                   (st, v_Bend_prev)],
                             inc=sv, tag=f"mev{H}_{seg}_{fk}_{piece}")
                        aslot_free[sl] = sv.n
                        m_ev_last = sv.n
                    if fk + 2 < FK:
                        emit_wchunk(fk + 2)

                for ts_ in range(16):
                    sl = ts_ % 2
                    ps = psA if sl == 0 else psB
                    first_w = [(sv, m_ev_last), (dwd, v_wd), (sv, yslot_free[sl])]
                    for fk in range(FK):
                        for dq in range(2):
                            def b_B(e, ps=ps, dq=dq, fk=fk, ts_=ts_):
                                return e.matmul(
                                    ps[:, dq * 512: dq * 512 + 512],
                                    sb["m_T"][:, fk * 2048 + ts_ * 128: fk * 2048 + ts_ * 128 + 128],
                                    sb["wd"][:, fk * 1024 + dq * 512: fk * 1024 + dq * 512 + 512],
                                    start=(fk == 0), stop=(fk == FK - 1))
                            P.op("tensor", b_B,
                                 waits=first_w if (fk == 0 and dq == 0) else (),
                                 inc=st if (fk == FK - 1 and dq == 1) else None,
                                 tag=f"B{H}_{seg}_{ts_}")
                    v_B = st.n
                    v_Bend_prev = v_B

                    if seg < 8:
                        def b_y(e, ps=ps, ts_=ts_, seg=seg):
                            return e.scalar_tensor_tensor(
                                sb["y_acc"][:, ts_ * 1024:(ts_ + 1) * 1024],
                                ps[:, 0:1024],
                                sb["comb"][:, ts_ * 8 + seg: ts_ * 8 + seg + 1],
                                sb["y_acc"][:, ts_ * 1024:(ts_ + 1) * 1024],
                                OP.mult, OP.add)
                    else:
                        def b_y(e, ps=ps, ts_=ts_):
                            return e.tensor_add(
                                sb["y_acc"][:, ts_ * 1024:(ts_ + 1) * 1024],
                                sb["y_acc"][:, ts_ * 1024:(ts_ + 1) * 1024],
                                ps[:, 0:1024])
                    P.op("vector", b_y,
                         waits=[(st, v_B), (sv, v_comb), (dy[ts_ % 8], yinit_dx[ts_])],
                         inc=sv, tag=f"yev{H}_{seg}_{ts_}")
                    yslot_free[sl] = sv.n
                    if seg == nseg - 1:
                        v_yev = sv.n

                        def b_out(e, H=H, ts_=ts_):
                            r0 = H * S + ts_ * 128
                            return e.dma_start(out_t.ap()[r0:r0 + 128, :],
                                               sb["y_acc"][:, ts_ * 1024:(ts_ + 1) * 1024])
                        P.op("sync", b_out, waits=[(sv, v_yev)], inc=dout, amt=16,
                             tag=f"out{H}_{ts_}")
            v_y_all = sv.n
            v_prev_half_st = st.n
            v_out_prev = dout.n

    if dump is not None:
        if dump in ("stats", "gstats", "comb", "lgbuf"):
            w = sb[dump].ap().shape[1]
            def b_dump(e, dump=dump, w=w):
                return e.dma_start(out_t.ap()[0:128, 0:w], sb[dump][:, 0:w])
        else:
            def b_dump(e, dump=dump):
                return e.dma_start(
                    bass.AP(out_t, 0, [[D, 128], [128 * D, 16], [1, D]]),
                    sb[dump][:, :].rearrange("p (t d) -> p t d", t=16))
        P.op("gpsimd", b_dump,
             waits=[(ss, ss.n), (sv, sv.n), (st, st.n)]
                   + [(s, s.n) for s in dht + dstr + dnat + dx + dy]
                   + [(dsp[0], dsp[0].n), (dsp[1], dsp[1].n)],
             inc=dout, amt=16, tag="dump")

    P.op("sync", None, waits=[(dsp[0], dsp[0].n), (dsp[1], dsp[1].n),
                              (dout, dout.n)], tag="final_wait")

    P.check()

    import contextlib
    with contextlib.ExitStack() as ctx:
        for s in P.sems:
            s.h = ctx.enter_context(nc.semaphore(s.name))
        with nc.Block() as block:
            for eng_name in Plan.ENGS:
                ops = P.ops[eng_name]

                def body(e, ops=ops):
                    for f in ops:
                        f(e)
                getattr(block, eng_name)(body)
    probe = nc.alloc_sbuf_tensor("_guard_probe", [128, 4], mybir.dt.uint8)
    paddr = nc.lookup_mloc(probe).addr
    assert paddr + 4 <= SB0, f"const pool collided with kernel map: {paddr} >= {SB0}"
    return nc


_NC_CACHE = {}


def _get_nc(key="full", **kw):
    if key not in _NC_CACHE:
        _NC_CACHE[key] = _build_program(**kw)
    return _NC_CACHE[key]


def kernel(L0_attn_qz, L0_attn_ternary_factor_u, L0_attn_ternary_factor_v,
           L0_ffn_gate_w, L0_ffn_w_gate, L0_ffn_w_up, L0_ffn_w_down,
           L0_ffn_sw_gate, L0_ffn_sw_up, L0_ffn_sw_down):
    qz = np.ascontiguousarray(np.asarray(L0_attn_qz, dtype=np.float32))
    u = np.ascontiguousarray(np.asarray(L0_attn_ternary_factor_u, dtype=np.float32))
    v = np.ascontiguousarray(np.asarray(L0_attn_ternary_factor_v, dtype=np.float32))
    gw = np.ascontiguousarray(np.asarray(L0_ffn_gate_w, dtype=np.float32))
    wg = np.ascontiguousarray(np.asarray(L0_ffn_w_gate, dtype=np.float32))
    wu = np.ascontiguousarray(np.asarray(L0_ffn_w_up, dtype=np.float32))
    wd = np.ascontiguousarray(np.asarray(L0_ffn_w_down, dtype=np.float32))
    swg = np.ascontiguousarray(np.asarray(L0_ffn_sw_gate, dtype=np.float32))
    swu = np.ascontiguousarray(np.asarray(L0_ffn_sw_up, dtype=np.float32))
    swd = np.ascontiguousarray(np.asarray(L0_ffn_sw_down, dtype=np.float32))

    nc = _get_nc()
    in_maps = []
    for i in range(NCORE):
        xs = np.ascontiguousarray(qz[BPC * i: BPC * (i + 1)].reshape(T, D))
        in_maps.append(dict(x=xs, u=u, v=v, gw=gw, wg=wg, wu=wu, wd=wd,
                            swg=swg, swu=swu, swd=swd))
    res = run_bass_kernel_spmd(nc, in_maps, core_ids=list(range(NCORE)))
    shards = [res.results[i]["out"].reshape(BPC, S, D) for i in range(NCORE)]
    return np.concatenate(shards, axis=0)


# version 13 (final)
# speedup vs baseline: 1.0863x; 1.0259x over previous
import numpy as np
import concourse.bass as bass
import concourse.mybir as mybir
from concourse.bass_utils import run_bass_kernel_spmd

F32 = mybir.dt.float32
BF16 = mybir.dt.bfloat16
AF = mybir.ActivationFunctionType
OP = mybir.AluOpType
AX = mybir.AxisListType

NCORE = 8
BPC = 2
S = 2048
D = 1024
C = 16
T = BPC * S
E = 8
FF = 1408
NSEG = 10
FK = FF // 128
SCALE = 1.0 / 32.0
MOE_SCALE = 2.5
EPS = 1e-5

SBUF_TOP = 229344
MAP_BYTES = 11264 + 196608
SB0 = SBUF_TOP - MAP_BYTES
PB = SB0 + 11264

PERM = dict(
    id32=(0 + SB0, [128, 128], F32),
    iota_r=(512 + SB0, [128, 128], F32),
    iota_c=(1024 + SB0, [128, 1], F32),
    gw_nat=(1056 + SB0, [8, 1024], F32),
    gwT=(5152 + SB0, [128, 64], F32),
    stats=(5408 + SB0, [128, 128], F32),
    gstats=(5920 + SB0, [128, 64], F32),
    lgbuf=(6176 + SB0, [128, 128], F32),
    comb=(6688 + SB0, [128, 128], F32),
    gtmp1=(7200 + SB0, [128, 128], F32),
    gtmp2=(7712 + SB0, [128, 128], F32),
    getmp=(8224 + SB0, [128, 128], F32),
    h2p32=(8736 + SB0, [128, 256], F32),
    gwb=(8736 + SB0, [128, 64], BF16),
    zbuf=(9760 + SB0, [1, 640], BF16),
)

ATTN = dict(
    A1=(PB + 0, [128, 16384], BF16),
    A2=(PB + 32768, [128, 16384], BF16),
    h_T=(PB + 65536, [128, 16384], BF16),
    p_row0=(PB + 65536, [128, 2048], BF16),
    p_row1=(PB + 69632, [128, 2048], BF16),
    ptsG0=(PB + 73728, [128, 4096], BF16),
    ptsG1=(PB + 81920, [128, 4096], BF16),
    unatc0=(PB + 90112, [128, 1024], BF16),
    unatc1=(PB + 92160, [128, 1024], BF16),
    vnatc0=(PB + 94208, [128, 1024], BF16),
    vnatc1=(PB + 96256, [128, 1024], BF16),
    UT=(PB + 98304, [128, 8192], BF16),
    VT=(PB + 114688, [128, 8192], BF16),
    U_nat=(PB + 131072, [128, 8192], BF16),
    V_nat=(PB + 147456, [128, 8192], BF16),
    x_in0=(PB + 163840, [128, 1024], F32),
    x_in1=(PB + 167936, [128, 1024], F32),
    x_in2=(PB + 172032, [128, 1024], F32),
    x_in3=(PB + 176128, [128, 1024], F32),
    sq=(PB + 180224, [128, 1024], F32),
    h_t0=(PB + 184320, [128, 1024], BF16),
    h_t1=(PB + 186368, [128, 1024], BF16),
    xp_out0=(PB + 188416, [128, 1024], F32),
    xp_out1=(PB + 192512, [128, 1024], F32),
)

FFN = dict(
    y_acc=(PB + 0, [128, 16384], F32),
    h2T=(PB + 65536, [128, 16384], BF16),
    m_T=(PB + 98304, [128, 22528], BF16),
    wd=(PB + 143360, [128, 11264], BF16),
    wgr0=(PB + 165888, [128, 1024], BF16),
    wgr1=(PB + 167936, [128, 1024], BF16),
    wur0=(PB + 169984, [128, 1024], BF16),
    wur1=(PB + 172032, [128, 1024], BF16),
    silu0=(PB + 174080, [128, 1024], BF16),
    silu1=(PB + 176128, [128, 1024], BF16),
    h2f0=(PB + 178176, [128, 1024], F32),
    h2f1=(PB + 182272, [128, 1024], F32),
    fsq=(PB + 186368, [128, 1024], F32),
    fh2b0=(PB + 190464, [128, 1024], BF16),
    fh2b1=(PB + 192512, [128, 1024], BF16),
)


class Sem:
    __slots__ = ("name", "n", "h")

    def __init__(self, name):
        self.name, self.n, self.h = name, 0, None


class Plan:
    ENGS = ("tensor", "vector", "scalar", "gpsimd", "sync")

    def __init__(self):
        self.ops = {e: [] for e in self.ENGS}
        self.sems = []
        self._waited = {e: {} for e in self.ENGS}

    def sem(self, name):
        s = Sem(name)
        self.sems.append(s)
        return s

    def op(self, eng, build, waits=(), inc=None, amt=1, tag=""):
        wl = []
        seen = self._waited[eng]
        for (s, v) in waits:
            if v is None or v <= 0:
                continue
            if seen.get(s.name, -1) >= v:
                continue
            seen[s.name] = v
            wl.append((s, v))
        if inc is not None:
            inc.n += amt
        post = inc.n if inc else None

        def f(e, _b=build, _w=wl, _i=inc, _a=amt):
            for s, v in _w:
                e.wait_ge(s.h, v)
            if _b is not None:
                ins = _b(e)
                if _i is not None:
                    ins.then_inc(_i.h, _a)

        f.tag = tag
        f.waits = wl
        f.inc = (inc, amt) if inc else None
        self.ops[eng].append(f)
        return post

    def check(self):
        cnt = {s.name: 0 for s in self.sems}
        idx = {e: 0 for e in self.ENGS}
        progress = True
        while progress:
            progress = False
            for e in self.ENGS:
                while idx[e] < len(self.ops[e]):
                    f = self.ops[e][idx[e]]
                    if all(cnt[s.name] >= v for s, v in f.waits):
                        if f.inc:
                            cnt[f.inc[0].name] += f.inc[1]
                        idx[e] += 1
                        progress = True
                    else:
                        break
        stuck = {e: len(self.ops[e]) - idx[e] for e in self.ENGS if idx[e] < len(self.ops[e])}
        if stuck:
            msg = []
            for e in stuck:
                f = self.ops[e][idx[e]]
                blocked = [(s.name, v, cnt[s.name]) for s, v in f.waits if cnt[s.name] < v]
                msg.append(f"{e}[{idx[e]}] tag={f.tag} blocked on {blocked}")
            raise RuntimeError("plan deadlock:\n" + "\n".join(msg))


def _build_program(do_attn=True, do_ffn=True, heads=C, nseg=NSEG, debug_xp_out=False,
                   bmax=BPC, do_proj=True, do_g=True, dump=None, ffn_halves=BPC):
    nc = bass.Bass("TRN2", target_bir_lowering=False)

    x_t = nc.dram_tensor("x", [T, D], F32, kind="ExternalInput")
    u_t = nc.dram_tensor("u", [D, D], F32, kind="ExternalInput")
    v_t = nc.dram_tensor("v", [D, D], F32, kind="ExternalInput")
    gw_t = nc.dram_tensor("gw", [E, D], F32, kind="ExternalInput")
    wg_t = nc.dram_tensor("wg", [E, D, FF], F32, kind="ExternalInput")
    wu_t = nc.dram_tensor("wu", [E, D, FF], F32, kind="ExternalInput")
    wd_t = nc.dram_tensor("wd", [E, FF, D], F32, kind="ExternalInput")
    swg_t = nc.dram_tensor("swg", [D, 2 * FF], F32, kind="ExternalInput")
    swu_t = nc.dram_tensor("swu", [D, 2 * FF], F32, kind="ExternalInput")
    swd_t = nc.dram_tensor("swd", [2 * FF, D], F32, kind="ExternalInput")
    out_t = nc.dram_tensor("out", [T, D], F32, kind="ExternalOutput")
    xp_t = nc.dram_tensor("xp", [T, D], F32)

    sb = {}
    for table in (PERM, ATTN, FFN):
        for name, (off, shape, dt) in table.items():
            sb[name] = nc.alloc_sbuf_tensor_at(name, shape, dt, offset=off)

    psA = nc.alloc_psum_tensor("psA", [128, 2048], F32)
    psB = nc.alloc_psum_tensor("psB", [128, 2048], F32)

    P = Plan()
    dx = [P.sem(f"dx{j}") for j in range(4)]
    dy = [P.sem(f"dy{j}") for j in range(8)]
    dwg = [P.sem(f"dwg{j}") for j in range(2)]
    dwu = [P.sem(f"dwu{j}") for j in range(2)]
    dwd = P.sem("dwd")
    duv = P.sem("duv")
    dgw = P.sem("dgw")
    duvt = P.sem("duvt")
    dht = [P.sem(f"dht{j}") for j in range(2)]
    dnat = [P.sem(f"dnat{j}") for j in range(4)]
    dstr = [P.sem(f"dstr{j}") for j in range(4)]
    dsp = [P.sem(f"dsp{j}") for j in range(2)]
    dout = P.sem("dout")
    st = P.sem("st")
    sv = P.sem("sv")
    ss = P.sem("ss")
    sg = P.sem("sg")

    x = x_t.ap()
    xp = xp_t.ap()

    P.op("gpsimd", lambda e: e.iota(sb["iota_r"][:, :], [[1, 128]], channel_multiplier=0,
                                    allow_small_or_imprecise_dtypes=True), inc=sg, tag="iota_r")
    P.op("gpsimd", lambda e: e.iota(sb["iota_c"][:, :], [[0, 1]], channel_multiplier=1,
                                    allow_small_or_imprecise_dtypes=True), inc=sg, tag="iota_c")
    v_iota = sg.n
    P.op("vector", lambda e: e.tensor_scalar(sb["id32"][:, :], sb["iota_r"][:, :],
                                             sb["iota_c"][:, 0:1], None, OP.is_equal),
         waits=[(sg, v_iota)], inc=sv, tag="id32")
    v_id32 = sv.n
    P.op("vector", lambda e: e.memset(sb["zbuf"][:, :], 0.0), inc=sv, tag="zbuf")
    v_zbuf = sv.n

    P.op("gpsimd", lambda e: e.dma_start(
        sb["U_nat"][:, :].rearrange("p (c d) -> p c d", c=8),
        bass.AP(u_t, 0, [[D, 128], [128 * D, 8], [1, D]])), inc=duv, amt=16, tag="dma_Unat")
    P.op("gpsimd", lambda e: e.dma_start(
        sb["V_nat"][:, :].rearrange("p (c d) -> p c d", c=8),
        bass.AP(v_t, 0, [[D, 128], [128 * D, 8], [1, D]])), inc=duv, amt=16, tag="dma_Vnat")
    v_uvnat = duv.n

    P.op("sync", lambda e: e.dma_start(sb["gw_nat"][:, :], gw_t.ap()[:, :]),
         inc=dgw, amt=16, tag="dma_gw")
    v_gwload = dgw.n

    for mat, src, dst in (("u", "U_nat", "UT"), ("v", "V_nat", "VT")):
        for j in range(8):
            def b_t(e, src=src, dst=dst, j=j):
                return e.dma_start_transpose(
                    sb[dst][:, :].rearrange("p (kk c) -> p kk c", kk=8)[:, :, j * 128:(j + 1) * 128],
                    sb[src][:, j * 1024:(j + 1) * 1024])
            P.op("sync", b_t, waits=[(duv, v_uvnat)], inc=duvt, amt=16, tag=f"dmat_{dst}{j}")
    v_utvt = duvt.n

    for kk in range(8):
        def b_gwt(e, kk=kk):
            return e.transpose(psB[:, kk * 8:(kk + 1) * 8], sb["gw_nat"][0:8, kk * 128:(kk + 1) * 128],
                               sb["id32"][0:8, 0:8])
        P.op("tensor", b_gwt, waits=[(dgw, v_gwload), (sv, v_id32)], inc=st, tag=f"tr_gw{kk}")
    v_gwtr = st.n
    P.op("scalar", lambda e: e.activation(sb["gwT"][:, 0:64], psB[:, 0:64], AF.Copy),
         waits=[(st, v_gwtr)], inc=ss, tag="ev_gwT")
    v_gwT = ss.n
    P.op("vector", lambda e: e.tensor_copy(sb["gwb"][:, :], sb["gwT"][:, :]),
         waits=[(ss, v_gwT)], inc=sv, tag="gwb")
    v_gwb = sv.n

    for wz in range(192):
        P.op("tensor", lambda e: e.matmul(psA[:, 0:128], sb["id32"][:, :],
                                          sb["id32"][:, :], start=True, stop=True),
             waits=[(sv, v_id32)], tag=f"warm{wz}")

    def ln_chain(i, src_ap_fn, sqbuf, waits_first, tagp, rstd_done):
        s0 = (i % 16) * 8
        stats = sb["stats"]
        P.op("vector", lambda e: e.reduce_sum(stats[:, s0:s0 + 1], src_ap_fn(), AX.X, negate=True),
             waits=list(waits_first), inc=sv, tag=f"{tagp}{i}_sum")
        v1 = sv.n
        yield
        P.op("vector", lambda e: e.scalar_tensor_tensor(sb[sqbuf][:, :], src_ap_fn(), 1.0,
                                                        src_ap_fn(), OP.mult, OP.mult,
                                                        accum_out=stats[:, s0 + 1:s0 + 2]),
             inc=sv, tag=f"{tagp}{i}_sq")
        v2 = sv.n
        yield
        P.op("vector", lambda e: e.tensor_scalar(stats[:, s0 + 2:s0 + 3], stats[:, s0:s0 + 1],
                                                 1.0 / D, None, OP.mult),
             waits=[(sv, v1)], inc=sv, tag=f"{tagp}{i}_negm")
        v3 = sv.n
        yield
        P.op("vector", lambda e: e.tensor_scalar(stats[:, s0 + 3:s0 + 4], stats[:, s0 + 1:s0 + 2],
                                                 1.0 / D, EPS, OP.mult, OP.add),
             waits=[(sv, v2)], inc=sv, tag=f"{tagp}{i}_q")
        v4 = sv.n
        yield
        P.op("vector", lambda e: e.tensor_scalar(stats[:, s0 + 4:s0 + 5], stats[:, s0 + 2:s0 + 3],
                                                 stats[:, s0 + 2:s0 + 3], None, OP.mult),
             waits=[(sv, v3)], inc=sv, tag=f"{tagp}{i}_m2")
        v5 = sv.n
        yield
        P.op("vector", lambda e: e.tensor_sub(stats[:, s0 + 5:s0 + 6], stats[:, s0 + 3:s0 + 4],
                                              stats[:, s0 + 4:s0 + 5]),
             waits=[(sv, v5)], inc=sv, tag=f"{tagp}{i}_ve")
        v6 = sv.n
        yield
        P.op("vector", lambda e: e.reciprocal(stats[:, s0 + 6:s0 + 7], stats[:, s0 + 5:s0 + 6]),
             waits=[(sv, v6)], inc=sv, tag=f"{tagp}{i}_ri")
        v7 = sv.n
        yield
        P.op("scalar", lambda e: e.activation(stats[:, s0 + 7:s0 + 8], stats[:, s0 + 6:s0 + 7], AF.Sqrt),
             waits=[(sv, v7)], inc=ss, tag=f"{tagp}{i}_rstd")
        rstd_done[i] = ss.n

    def run_waves(gens):
        alive = list(gens)
        while alive:
            nxt = []
            for g in alive:
                try:
                    next(g)
                    nxt.append(g)
                except StopIteration:
                    pass
            alive = nxt

    def ln_apply(i, src_ap_fn, out_ap_fn, apply_waits, tagp):
        s0 = (i % 16) * 8
        stats = sb["stats"]
        P.op("vector", lambda e: e.tensor_scalar(
            out_ap_fn(), src_ap_fn(), stats[:, s0 + 2:s0 + 3], stats[:, s0 + 7:s0 + 8],
            OP.add, OP.mult), waits=list(apply_waits), inc=sv, tag=f"{tagp}{i}_app")
        return sv.n

    xin_free = [0, 0, 0, 0]
    xin_load = [0, 0, 0, 0]
    xpout_free = [0, 0]
    v_G_end = 0
    v_Gevac_last = 0
    v_spill = [[0, 0], [0, 0]]
    v_heads_end = 0
    v_strips_last = [0, 0, 0, 0]
    v_natc_last = [0, 0, 0, 0]
    v_ht_slot = [0, 0]
    v_sig_all = 0

    pieces = [(psA, 0), (psA, 1024), (psB, 0), (psB, 1024)]

    hT_done = {}

    def emit_attn_ln_tile_tail(b, i, rstd_done, prev_heads_end, prev_strips, prev_natc):
        slot, hs = i % 4, i % 2
        ap_done = ln_apply(i, (lambda s=slot: sb[f"x_in{s}"][:, :]),
                           (lambda s=hs: sb[f"h_t{s}"][:, :]),
                           [(ss, rstd_done[i]), (dht[hs], v_ht_slot[hs])], f"lnA{b}_")
        xin_free[slot] = ap_done

        def b_ttr(e, i=i, s=hs):
            return e.dma_start_transpose(
                sb["h_T"][:, :].rearrange("p (kk t) -> p kk t", kk=8)[:, :, i * 128:(i + 1) * 128],
                sb[f"h_t{s}"][:, :])
        P.op("sync", b_ttr, waits=[(sv, ap_done), (st, prev_heads_end)]
             + [(dstr[j], prev_strips[j]) for j in range(4)]
             + [(dnat[j], prev_natc[j]) for j in range(4)],
             inc=dht[hs], amt=16, tag=f"dmat_hT_b{b}_{i}")
        v_ht_slot[hs] = dht[hs].n
        hT_done[(b, i)] = dht[hs].n

    def emit_attn_ln_x_load(b, i):
        slot = i % 4
        r0 = b * S + i * 128

        def b_ldx(e, r0=r0, slot=slot):
            return e.dma_start(sb[f"x_in{slot}"][:, :], x[r0:r0 + 128, :])
        P.op("sync", b_ldx, waits=[(sv, xin_free[slot])], inc=dx[slot], amt=16,
             tag=f"dma_x_b{b}_{i}")
        xin_load[slot] = dx[slot].n

    def emit_attn_ln_group(b, g4, prev_heads_end, prev_strips, prev_natc):
        tiles = [4 * g4 + k for k in range(4)]
        rstd_done = {}
        gens = []
        for i in tiles:
            emit_attn_ln_x_load(b, i)
            gens.append(ln_chain(i, (lambda s=i % 4: sb[f"x_in{s}"][:, :]), "sq",
                                 [(dx[i % 4], xin_load[i % 4])], f"lnA{b}_", rstd_done))
        run_waves(gens)
        for k, i in enumerate(tiles):
            emit_attn_ln_tile_tail(b, i, rstd_done, prev_heads_end, prev_strips, prev_natc)

    def emit_attn_ln_pair(b, i0, prev_heads_end, prev_strips, prev_natc):
        rstd_done = {}
        gens = []
        for i in (i0, i0 + 1):
            emit_attn_ln_x_load(b, i)
            gens.append(ln_chain(i, (lambda s=i % 4: sb[f"x_in{s}"][:, :]), "sq",
                                 [(dx[i % 4], xin_load[i % 4])], f"lnA{b}_", rstd_done))
        run_waves(gens)
        for i in (i0, i0 + 1):
            emit_attn_ln_tile_tail(b, i, rstd_done, prev_heads_end, prev_strips, prev_natc)

    if do_attn:
        for g4 in range(4):
            emit_attn_ln_group(0, g4, 0, [0, 0, 0, 0], [0, 0, 0, 0])
        for b in range(bmax):
            qpieces = [(psA, 0), (psA, 512), (psA, 1024), (psA, 1536),
                       (psB, 0), (psB, 512), (psB, 1024), (psB, 1536)]
            qpiece_free = [0] * 8
            for q in range(4):
                if not do_proj:
                    break
                for im, (wbuf, obuf) in enumerate((("UT", "A1"), ("VT", "A2"))):
                    for d_out in range(8):
                        g = q * 16 + im * 8 + d_out
                        ps, off = qpieces[g % 8]
                        first_waits = [(dht[(q * 4 + 3) % 2], hT_done[(b, q * 4 + 3)]),
                                       (dht[(q * 4 + 2) % 2], hT_done[(b, q * 4 + 2)]),
                                       (duvt, v_utvt),
                                       (ss, qpiece_free[g % 8]),
                                       (sv, v_Gevac_last), (ss, v_gwT)]
                        for kk in range(8):
                            def b_mm(e, ps=ps, off=off, kk=kk, wbuf=wbuf,
                                     d_out=d_out, q=q):
                                return e.matmul(
                                    ps[:, off: off + 512],
                                    sb[wbuf][:, kk * 1024 + d_out * 128: kk * 1024 + d_out * 128 + 128],
                                    sb["h_T"][:, kk * 2048 + q * 512: kk * 2048 + q * 512 + 512],
                                    start=(kk == 0), stop=(kk == 7))
                            P.op("tensor", b_mm,
                                 waits=first_waits if kk == 0 else (),
                                 inc=st if kk == 7 else None,
                                 tag=f"proj{b}_{g}_{kk}")
                        v_g = st.n

                        def b_ev(e, ps=ps, off=off, obuf=obuf, d_out=d_out, q=q):
                            return e.activation(
                                sb[obuf][:, d_out * 2048 + q * 512: d_out * 2048 + q * 512 + 512],
                                ps[:, off:off + 512], AF.Copy, scale=SCALE)
                        P.op("scalar", b_ev, waits=[(st, v_g)], inc=ss, tag=f"projev{b}_{g}")
                        qpiece_free[g % 8] = ss.n
            v_proj_ev = ss.n

            natc_free = [0, 0]
            natc_val = [None, None]
            oevac_prev = 0
            oevac_prev_ss = 0
            prow_free = [0, 0]
            prow_free_dt = [[], []]
            ptsg_free = [0, 0]
            fpair_free = [0, 0]

            def emit_natc(c):
                cc2 = c % 2
                vals = []
                for sidx, (srcb, dstb) in enumerate((("A1", f"unatc{cc2}"), ("A2", f"vnatc{cc2}"))):
                    def b_nat(e, srcb=srcb, dstb=dstb, base=cc2 * 64, chunk=c // 2):
                        return e.dma_start_transpose(
                            sb[dstb][:, :].rearrange("p (j r) -> p j r", j=16),
                            sb[srcb][base:base + 64, chunk * 2048:(chunk + 1) * 2048])
                    sem = dnat[sidx * 2 + cc2]
                    P.op("sync", b_nat, waits=[(ss, v_proj_ev), (st, natc_free[cc2])],
                         inc=sem, amt=16, tag=f"nat{b}_{c}")
                    vals.append((sem, sem.n))
                natc_val[cc2] = vals

            emit_natc(0)
            for c in range(heads):
                cc = c % 2
                base = cc * 64
                chunk = c // 2
                v_natc = natc_val[cc]

                v_zfill = None
                sig_done = {}
                strip_done = {}
                o1_done = {}
                o2_done = {}

                def emit_o2(i):
                    sl = i % 2
                    for tq in range(4):
                        w = [(ss, sig_done[(i, tq // 2)])]
                        if i == 0 and tq == 0:
                            w += [(st, v_zfill)] + list(v_natc)

                        def b_o2(e, sl=sl, tq=tq, i=i, cc=cc):
                            return e.matmul(
                                psB[64:128, tq * 512: tq * 512 + 512],
                                sb[f"unatc{cc}"][:, i * 64: i * 64 + 64],
                                sb[f"p_row{sl}"][:, tq * 512: tq * 512 + 512],
                                start=False, stop=(i == 15), tile_position=(0, 64),
                                skip_group_check=True)
                        P.op("tensor", b_o2, waits=w, inc=st if tq == 3 else None,
                             tag=f"o2_{b}_{c}_{i}_{tq}")
                    o2_done[i] = st.n
                    prow_free[sl] = st.n

                def emit_o1(g):
                    for j in range(16):
                        i1 = 2 * g + 1
                        w = [(dstr[i1 % 4], strip_done[i1]),
                             (dstr[(i1 - 1) % 4], strip_done[i1 - 1])]
                        if g == 0 and j == 0:
                            w += [(st, v_zfill)] + list(v_natc)

                        def b_o1(e, g=g, j=j, cc=cc):
                            return e.matmul(
                                psB[0:64, g * 256: g * 256 + 256],
                                sb[f"vnatc{cc}"][:, j * 64: j * 64 + 64],
                                sb[f"ptsG{g % 2}"][:, j * 256: j * 256 + 256],
                                start=False, stop=(j == 15), tile_position=(0, 0),
                                skip_group_check=True)
                        P.op("tensor", b_o1, waits=w, inc=st if j == 15 else None,
                             tag=f"o1_{b}_{c}_{g}_{j}")
                    o1_done[g] = st.n
                    ptsg_free[g % 2] = st.n

                for i in range(16):
                    sl = i % 2
                    for tq in range(4):
                        h = tq // 2

                        def b_F(e, h=h, tq=tq, base=base, chunk=chunk, i=i):
                            return e.matmul(
                                psA[:, h * 1024 + (tq % 2) * 512: h * 1024 + (tq % 2) * 512 + 512],
                                sb["A1"][base:base + 64, chunk * 2048 + i * 128: chunk * 2048 + i * 128 + 128],
                                sb["A2"][base:base + 64, chunk * 2048 + tq * 512: chunk * 2048 + tq * 512 + 512],
                                start=True, stop=True, tile_position=(base, 0))
                        P.op("tensor", b_F,
                             waits=[(ss, fpair_free[h]), (ss, v_proj_ev)],
                             inc=st if tq % 2 == 1 else None, tag=f"F{b}_{c}_{i}_{tq}")
                        if tq % 2 == 1:
                            v_fpair = st.n
                            def b_sig(e, h=h, sl=sl):
                                return e.activation(
                                    sb[f"p_row{sl}"][:, h * 1024: h * 1024 + 1024],
                                    psA[:, h * 1024: h * 1024 + 1024], AF.Sigmoid)
                            P.op("scalar", b_sig,
                                 waits=[(st, v_fpair), (st, prow_free[sl])]
                                       + prow_free_dt[sl],
                                 inc=ss, tag=f"sig{b}_{c}_{i}_{h}")
                            sig_done[(i, h)] = ss.n
                            fpair_free[h] = ss.n

                    if i == 1:
                        for zq in range(4):
                            def b_z(e, zq=zq):
                                return e.matmul(psB[:, zq * 512: zq * 512 + 512],
                                                sb["zbuf"][0:1, 0:128], sb["zbuf"][0:1, 128:640],
                                                start=True, stop=False, skip_group_check=True)
                            P.op("tensor", b_z, waits=[(sv, oevac_prev), (ss, oevac_prev_ss),
                                                       (sv, v_zbuf), (ss, v_proj_ev)],
                                 inc=st if zq == 3 else None, tag=f"zfill{b}_{c}_{zq}")
                        v_zfill = st.n
                    if i == 8 and c + 1 < heads:
                        emit_natc(c + 1)
                    if i >= 1:
                        emit_o2(i - 1)
                    if i >= 4 and i % 2 == 0:
                        emit_o1((i - 4) // 2)

                    g = i // 2

                    def b_strip(e, sl=sl, g=g, i=i):
                        return e.dma_start_transpose(
                            sb[f"ptsG{g % 2}"][:, :]
                            .rearrange("p (j s) -> p j s", j=16)[:, :, (i % 2) * 128:(i % 2) * 128 + 128],
                            sb[f"p_row{sl}"][:, :])
                    P.op("sync", b_strip, waits=[(ss, sig_done[(i, 1)]),
                                                 (st, ptsg_free[g % 2])],
                         inc=dstr[i % 4], amt=16, tag=f"strip{b}_{c}_{i}")
                    strip_done[i] = dstr[i % 4].n
                    prow_free_dt[sl] = [(dstr[i % 4], dstr[i % 4].n)]


                emit_o2(15)
                emit_o1(6)
                emit_o1(7)
                v_oend = max(o2_done[15], o1_done[7])
                v_heads_end = st.n
                natc_free[cc] = st.n

                def b_ev1(e, base=base, chunk=chunk):
                    return e.tensor_copy(sb["A1"][base:base + 64, chunk * 2048:(chunk + 1) * 2048],
                                         psB[0:64, :])
                P.op("vector", b_ev1, waits=[(st, v_oend)] + list(v_natc), inc=sv,
                     tag=f"oev1_{b}_{c}")

                def b_ev2(e, base=base, chunk=chunk):
                    return e.tensor_copy(sb["A2"][base:base + 64, chunk * 2048:(chunk + 1) * 2048],
                                         psB[64:128, :])
                P.op("vector", b_ev2, inc=sv, tag=f"oev2_{b}_{c}")
                oevac_prev = sv.n
                oevac_prev_ss = 0
            v_oevac_all = sv.n
            v_strips_last = [s.n for s in dstr]
            v_natc_last = [s.n for s in dnat]
            v_sig_all = ss.n

            gpiece_free = [0, 0, 0, 0]
            for t in range(16 if do_g else 0):
                ps, off = pieces[t % 4]
                slot = t % 4
                r0 = b * S + t * 128

                def b_ldx2(e, r0=r0, slot=slot):
                    return e.dma_start(sb[f"x_in{slot}"][:, :], x[r0:r0 + 128, :])
                P.op("sync", b_ldx2, waits=[(sv, xin_free[slot])], inc=dx[slot], amt=16,
                     tag=f"dma_xG_b{b}_{t}")
                xin_load[slot] = dx[slot].n

                first_w = [(sv, v_oevac_all), (ss, oevac_prev_ss), (duv, v_uvnat),
                           (sv, gpiece_free[t % 4]), (ss, v_proj_ev), (ss, v_sig_all)]
                for kk in range(16):
                    src_o = "A1" if kk < 8 else "A2"
                    src_w = "U_nat" if kk < 8 else "V_nat"
                    k8 = kk % 8
                    for dq in range(2):
                        def b_G(e, ps=ps, off=off, dq=dq, src_o=src_o, src_w=src_w,
                                k8=k8, t=t, kk=kk):
                            return e.matmul(
                                ps[:, off + dq * 512: off + dq * 512 + 512],
                                sb[src_o][:, k8 * 2048 + t * 128: k8 * 2048 + t * 128 + 128],
                                sb[src_w][:, k8 * 1024 + dq * 512: k8 * 1024 + dq * 512 + 512],
                                start=(kk == 0), stop=(kk == 15))
                        P.op("tensor", b_G, waits=first_w if (kk == 0 and dq == 0) else (),
                             inc=st if (kk == 15 and dq == 1) else None,
                             tag=f"G{b}_{t}_{kk}_{dq}")
                v_g = st.n
                v_G_end = v_g

                def b_gev(e, ps=ps, off=off, slot=slot, t=t):
                    return e.scalar_tensor_tensor(sb[f"xp_out{t % 2}"][:, :],
                                                  ps[:, off:off + 1024], SCALE,
                                                  sb[f"x_in{slot}"][:, :], OP.mult, OP.add)
                P.op("vector", b_gev, waits=[(st, v_g), (dx[slot], xin_load[slot]),
                                             (dsp[t % 2], xpout_free[t % 2])],
                     inc=sv, tag=f"gev{b}_{t}")
                xin_free[slot] = sv.n
                gpiece_free[t % 4] = sv.n
                v_gev = sv.n
                v_Gevac_last = v_gev

                def b_spill(e, r0=r0, t=t):
                    return e.dma_start(xp[r0:r0 + 128, :], sb[f"xp_out{t % 2}"][:, :])
                P.op("sync", b_spill, waits=[(sv, v_gev)], inc=dsp[t % 2], amt=16,
                     tag=f"spill{b}_{t}")
                xpout_free[t % 2] = dsp[t % 2].n
                v_spill[b] = [dsp[0].n, dsp[1].n]

                if b + 1 < bmax and t % 2 == 1:
                    emit_attn_ln_pair(b + 1, t - 1, v_heads_end, v_strips_last, v_natc_last)

        if debug_xp_out:
            for i in range(min(bmax * 16, T // 128)):
                def b_dbg(e, i=i):
                    return e.dma_start(out_t.ap()[i * 128:(i + 1) * 128, :], xp[i * 128:(i + 1) * 128, :])
                P.op("sync", b_dbg, waits=[(dsp[0], v_spill[bmax - 1][0]),
                                           (dsp[1], v_spill[bmax - 1][1])],
                     inc=dout, amt=16, tag=f"dbgout{i}")

    if do_ffn:
        v_out_prev = 0
        v_prev_half_st = 0
        wgr_free = [0, 0]
        wur_free = [0, 0]
        aslot_free = [0, 0]
        yslot_free = [0, 0]
        v_Bend_prev = 0
        for H in range(ffn_halves):
            w_yacc = [(st, v_G_end), (dsp[0], v_spill[H][0]), (dsp[1], v_spill[H][1]),
                      (dout, v_out_prev)]
            w_rA = ([(st, v_heads_end), (st, v_prev_half_st)]
                    + [(dstr[j], v_strips_last[j]) for j in range(4)]
                    + [(dnat[j], v_natc_last[j]) for j in range(4)])

            yinit_dx = [0] * 16
            tpiece_free = [0, 0, 0, 0]
            h2p_free = [0, 0]
            logits_done = [0] * 16
            v_h2f_free = [0, 0]
            v_fh2b_free = [0, 0]

            def emit_gate_tail(H, i, rstd_done):
                hs = i % 2
                ap_done = ln_apply(i, (lambda i=i: sb["y_acc"][:, i * 1024:(i + 1) * 1024]),
                                   (lambda s=hs: sb[f"h2f{s}"][:, :]),
                                   [(ss, rstd_done[i]), (sv, v_h2f_free[hs])], f"lnF{H}_")

                def b_cp(e, s=hs):
                    return e.tensor_copy(sb[f"fh2b{s}"][:, :], sb[f"h2f{s}"][:, :])
                P.op("vector", b_cp, waits=[(dht[hs], v_fh2b_free[hs]),
                                            (dsp[0], v_spill[BPC - 1][0]),
                                            (dsp[1], v_spill[BPC - 1][1])],
                     inc=sv, tag=f"h2b{H}_{i}")
                v_cp = sv.n
                v_h2f_free[hs] = v_cp

                def b_h2t(e, i=i, s=hs):
                    return e.dma_start_transpose(
                        sb["h2T"][:, :].rearrange("p (kk t) -> p kk t", kk=8)[:, :, i * 128:(i + 1) * 128],
                        sb[f"fh2b{s}"][:, :])
                P.op("sync", b_h2t, waits=[(sv, v_cp)] + w_rA, inc=dht[hs], amt=16,
                     tag=f"dmat_h2T_{H}_{i}")
                v_fh2b_free[hs] = dht[hs].n

                for wz in range(12):
                    P.op("tensor", lambda e: e.matmul(
                        psA[:, 0:128], sb["id32"][:, :], sb["id32"][:, :],
                        start=True, stop=True),
                         waits=[(sv, v_Gevac_last), (sv, yslot_free[0]),
                                (sv, yslot_free[1])], tag=f"gwarm{H}_{i}_{wz}")
                for kk in range(8):
                    def b_lg(e, i=i, kk=kk):
                        return e.matmul(psB[:, i * 8: i * 8 + 8],
                                        sb["h2T"][:, kk * 2048 + i * 128: kk * 2048 + i * 128 + 128],
                                        sb["gwb"][:, kk * 8: kk * 8 + 8],
                                        start=(kk == 0), stop=(kk == 7))
                    P.op("tensor", b_lg, waits=[(dht[hs], v_fh2b_free[hs]),
                                                (sv, v_gwb), (sv, v_Gevac_last),
                                                (sv, yslot_free[0]), (sv, yslot_free[1])],
                         inc=st, tag=f"lg{H}_{i}_{kk}")
                    if kk == 7:
                        logits_done[i] = st.n

            for g4 in range(4):
                tiles = [4 * g4 + k for k in range(4)]
                rstd_done = {}
                gens = []
                for i in tiles:
                    r0 = H * S + i * 128

                    def b_ldxp(e, r0=r0, i=i):
                        return e.dma_start(sb["y_acc"][:, i * 1024:(i + 1) * 1024], xp[r0:r0 + 128, :])
                    P.op("sync", b_ldxp,
                         waits=w_yacc + [(dy[i % 8], dy[i % 8].n)],
                         inc=dy[i % 8], amt=16, tag=f"dma_xp_h{H}_{i}")
                    yinit_dx[i] = dy[i % 8].n
                    gens.append(ln_chain(i, (lambda i=i: sb["y_acc"][:, i * 1024:(i + 1) * 1024]),
                                         "fsq",
                                         [(dy[i % 8], yinit_dx[i]),
                                          (dsp[0], v_spill[BPC - 1][0]),
                                          (dsp[1], v_spill[BPC - 1][1]),
                                          (dht[0], v_ht_slot[0]), (dht[1], v_ht_slot[1])],
                                         f"lnF{H}_", rstd_done))
                run_waves(gens)
                for i in tiles:
                    emit_gate_tail(H, i, rstd_done)
            v_h2T = [(dht[0], dht[0].n), (dht[1], dht[1].n)]
            v_logits_all = logits_done[15]

            gst = sb["gstats"]
            lg_ev = [0] * 16
            for i in range(16):
                P.op("vector", lambda e, i=i: e.tensor_copy(
                    sb["lgbuf"][:, i * 8:(i + 1) * 8], psB[:, i * 8:(i + 1) * 8]),
                    waits=[(st, v_logits_all)], inc=sv, tag=f"lgev{H}_{i}")
                lg_ev[i] = sv.n
            g1 = [0] * 16
            for i in range(16):
                P.op("vector", lambda e, i=i: e.reduce_max(
                    gst[:, i:i + 1], sb["lgbuf"][:, i * 8:(i + 1) * 8], AX.X, negate=True),
                    waits=[(sv, lg_ev[i])], inc=sv, tag=f"gm1_{H}_{i}")
                g1[i] = sv.n
            g2 = [0] * 16
            for i in range(16):
                P.op("vector", lambda e, i=i: e.tensor_scalar(
                    sb["gtmp1"][:, i * 8:(i + 1) * 8], sb["lgbuf"][:, i * 8:(i + 1) * 8],
                    gst[:, i:i + 1], 0.0, OP.add, OP.is_ge),
                    waits=[(sv, g1[i])], inc=sv, tag=f"gge_{H}_{i}")
                g2[i] = sv.n
            g3 = [0] * 16
            for i in range(16):
                P.op("vector", lambda e, i=i: e.scalar_tensor_tensor(
                    sb["gtmp2"][:, i * 8:(i + 1) * 8], sb["gtmp1"][:, i * 8:(i + 1) * 8],
                    -1e30, sb["lgbuf"][:, i * 8:(i + 1) * 8], OP.mult, OP.add),
                    waits=[(sv, g2[i])], inc=sv, tag=f"glg2_{H}_{i}")
                g3[i] = sv.n
            g4 = [0] * 16
            for i in range(16):
                P.op("vector", lambda e, i=i: e.reduce_max(
                    gst[:, 16 + i:17 + i], sb["gtmp2"][:, i * 8:(i + 1) * 8], AX.X, negate=True),
                    waits=[(sv, g3[i])], inc=sv, tag=f"gm2_{H}_{i}")
                g4[i] = sv.n
            g5 = [0] * 16
            for i in range(16):
                P.op("vector", lambda e, i=i: e.tensor_scalar(
                    sb["gtmp1"][:, i * 8:(i + 1) * 8], sb["lgbuf"][:, i * 8:(i + 1) * 8],
                    gst[:, 16 + i:17 + i], 0.0, OP.add, OP.is_ge),
                    waits=[(sv, g4[i])], inc=sv, tag=f"gmask_{H}_{i}")
                g5[i] = sv.n
            g6 = [0] * 16
            for i in range(16):
                P.op("scalar", lambda e, i=i: e.activation(
                    sb["getmp"][:, i * 8:(i + 1) * 8], sb["lgbuf"][:, i * 8:(i + 1) * 8],
                    AF.Exp, bias=gst[:, i:i + 1], scale=1.0),
                    waits=[(sv, g1[i])], inc=ss, tag=f"gexp_{H}_{i}")
                g6[i] = ss.n
            g7 = [0] * 16
            for i in range(16):
                P.op("vector", lambda e, i=i: e.tensor_mul(
                    sb["gtmp2"][:, i * 8:(i + 1) * 8], sb["getmp"][:, i * 8:(i + 1) * 8],
                    sb["gtmp1"][:, i * 8:(i + 1) * 8]),
                    waits=[(sv, g5[i]), (ss, g6[i])], inc=sv, tag=f"gw_{H}_{i}")
                g7[i] = sv.n
            g8 = [0] * 16
            for i in range(16):
                P.op("vector", lambda e, i=i: e.reduce_sum(
                    gst[:, i:i + 1], sb["gtmp2"][:, i * 8:(i + 1) * 8], AX.X),
                    waits=[(sv, g7[i])], inc=sv, tag=f"gsum_{H}_{i}")
                g8[i] = sv.n
            g9 = [0] * 16
            for i in range(16):
                P.op("vector", lambda e, i=i: e.reciprocal(
                    gst[:, 32 + i:33 + i], gst[:, i:i + 1]),
                    waits=[(sv, g8[i])], inc=sv, tag=f"grec_{H}_{i}")
                g9[i] = sv.n
            g10 = [0] * 16
            for i in range(16):
                P.op("vector", lambda e, i=i: e.tensor_scalar(
                    gst[:, 48 + i:49 + i], gst[:, 32 + i:33 + i], MOE_SCALE, None, OP.mult),
                    waits=[(sv, g9[i])], inc=sv, tag=f"gcsc_{H}_{i}")
                g10[i] = sv.n
            comb_done = [0] * 16
            for i in range(16):
                P.op("vector", lambda e, i=i: e.tensor_scalar(
                    sb["comb"][:, i * 8:(i + 1) * 8], sb["gtmp2"][:, i * 8:(i + 1) * 8],
                    gst[:, 48 + i:49 + i], None, OP.mult),
                    waits=[(sv, g10[i])], inc=sv, tag=f"gcmb_{H}_{i}")
                comb_done[i] = sv.n
            v_comb = comb_done[15]

            for seg in range(nseg):
                if seg < 8:
                    wgsrc, wusrc = wg_t, wu_t
                    wg_base = seg * D * FF
                    wg_row = FF
                    wdsrc, wd_base = wd_t, seg * FF * D
                else:
                    wgsrc, wusrc = swg_t, swu_t
                    wg_base = (seg - 8) * FF
                    wg_row = 2 * FF
                    wdsrc, wd_base = swd_t, (seg - 8) * FF * D

                wg_ready = {}
                wu_ready = {}

                def emit_wchunk(fk):
                    for (wsrc, ring, freetrk, rdy, sems) in (
                            (wgsrc, "wgr", wgr_free, wg_ready, dwg),
                            (wusrc, "wur", wur_free, wu_ready, dwu)):
                        def b_wg(e, wsrc=wsrc, ring=ring, fk=fk, wg_base=wg_base, wg_row=wg_row):
                            return e.dma_start(
                                sb[f"{ring}{fk % 2}"][:, :].rearrange("p (j f) -> p j f", j=8),
                                bass.AP(wsrc, wg_base + fk * 128,
                                        [[wg_row, 128], [128 * wg_row, 8], [1, 128]]))
                        sem = sems[fk % 2]
                        P.op("gpsimd", b_wg,
                             waits=[(st, freetrk[fk % 2]), (sv, v_Gevac_last)],
                             inc=sem, amt=16, tag=f"dma_w_{H}_{seg}_{fk}_{ring}")
                        rdy[fk] = (sem, sem.n)

                emit_wchunk(0)
                emit_wchunk(1)

                def b_wd(e, wdsrc=wdsrc, wd_base=wd_base):
                    return e.dma_start(
                        sb["wd"][:, :].rearrange("p (fk d) -> p fk d", fk=11),
                        bass.AP(wdsrc, wd_base, [[D, 128], [128 * D, 11], [1, D]]))
                P.op("gpsimd", b_wd,
                     waits=[(st, v_Bend_prev), (st, v_G_end), (sv, v_Gevac_last)],
                     inc=dwd, amt=16, tag=f"dma_wd_{H}_{seg}")
                v_wd = dwd.n

                m_ev_last = 0
                for fk in range(FK):
                    for piece in range(2):
                        sl = (fk * 2 + piece) % 2
                        ps = psA if sl == 0 else psB
                        first_w = [wg_ready[fk]] + list(v_h2T) + [
                                   (sv, aslot_free[sl]), (st, v_logits_all),
                                   (ss, tpiece_free[0]), (ss, tpiece_free[1]),
                                   (ss, tpiece_free[2]), (ss, tpiece_free[3]),
                                   (sv, lg_ev[15]), (sv, yslot_free[sl])]
                        for dd in range(8):
                            for tq in range(2):
                                def b_gA(e, ps=ps, tq=tq, fk=fk, dd=dd, piece=piece):
                                    return e.matmul(
                                        ps[:, tq * 512: tq * 512 + 512],
                                        sb[f"wgr{fk % 2}"][:, dd * 128: dd * 128 + 128],
                                        sb["h2T"][:, dd * 2048 + piece * 1024 + tq * 512:
                                                  dd * 2048 + piece * 1024 + tq * 512 + 512],
                                        start=(dd == 0), stop=(dd == 7))
                                P.op("tensor", b_gA,
                                     waits=first_w if (dd == 0 and tq == 0) else (),
                                     inc=st if (dd == 7 and tq == 1) else None,
                                     tag=f"Ag{H}_{seg}_{fk}_{piece}")
                        v_gate = st.n
                        wgr_free[fk % 2] = v_gate
                        for dd in range(8):
                            for tq in range(2):
                                def b_uA(e, ps=ps, tq=tq, fk=fk, dd=dd, piece=piece):
                                    return e.matmul(
                                        ps[:, 1024 + tq * 512: 1024 + tq * 512 + 512],
                                        sb[f"wur{fk % 2}"][:, dd * 128: dd * 128 + 128],
                                        sb["h2T"][:, dd * 2048 + piece * 1024 + tq * 512:
                                                  dd * 2048 + piece * 1024 + tq * 512 + 512],
                                        start=(dd == 0), stop=(dd == 7))
                                P.op("tensor", b_uA,
                                     waits=[wu_ready[fk]] if (dd == 0 and tq == 0) else (),
                                     inc=st if (dd == 7 and tq == 1) else None,
                                     tag=f"Au{H}_{seg}_{fk}_{piece}")
                        v_up = st.n
                        wur_free[fk % 2] = v_up

                        def b_si(e, ps=ps, sl=sl):
                            return e.activation(sb[f"silu{sl}"][:, :], ps[:, 0:1024], AF.Silu)
                        P.op("scalar", b_si, waits=[(st, v_gate), (sv, v_Gevac_last),
                                                    (sv, aslot_free[sl])],
                             inc=ss, tag=f"si{H}_{seg}_{fk}_{piece}")
                        v_si = ss.n

                        def b_m(e, ps=ps, sl=sl, fk=fk, piece=piece):
                            return e.tensor_mul(
                                sb["m_T"][:, fk * 2048 + piece * 1024: fk * 2048 + piece * 1024 + 1024],
                                sb[f"silu{sl}"][:, :], ps[:, 1024:2048])
                        P.op("vector", b_m, waits=[(ss, v_si), (st, v_up), (st, v_G_end),
                                                   (st, v_Bend_prev)],
                             inc=sv, tag=f"mev{H}_{seg}_{fk}_{piece}")
                        aslot_free[sl] = sv.n
                        m_ev_last = sv.n
                    if fk + 2 < FK:
                        emit_wchunk(fk + 2)

                for ts_ in range(16):
                    sl = ts_ % 2
                    ps = psA if sl == 0 else psB
                    first_w = [(sv, m_ev_last), (dwd, v_wd), (sv, yslot_free[sl])]
                    for fk in range(FK):
                        for dq in range(2):
                            def b_B(e, ps=ps, dq=dq, fk=fk, ts_=ts_):
                                return e.matmul(
                                    ps[:, dq * 512: dq * 512 + 512],
                                    sb["m_T"][:, fk * 2048 + ts_ * 128: fk * 2048 + ts_ * 128 + 128],
                                    sb["wd"][:, fk * 1024 + dq * 512: fk * 1024 + dq * 512 + 512],
                                    start=(fk == 0), stop=(fk == FK - 1))
                            P.op("tensor", b_B,
                                 waits=first_w if (fk == 0 and dq == 0) else (),
                                 inc=st if (fk == FK - 1 and dq == 1) else None,
                                 tag=f"B{H}_{seg}_{ts_}")
                    v_B = st.n
                    v_Bend_prev = v_B

                    if seg < 8:
                        def b_y(e, ps=ps, ts_=ts_, seg=seg):
                            return e.scalar_tensor_tensor(
                                sb["y_acc"][:, ts_ * 1024:(ts_ + 1) * 1024],
                                ps[:, 0:1024],
                                sb["comb"][:, ts_ * 8 + seg: ts_ * 8 + seg + 1],
                                sb["y_acc"][:, ts_ * 1024:(ts_ + 1) * 1024],
                                OP.mult, OP.add)
                    else:
                        def b_y(e, ps=ps, ts_=ts_):
                            return e.tensor_add(
                                sb["y_acc"][:, ts_ * 1024:(ts_ + 1) * 1024],
                                sb["y_acc"][:, ts_ * 1024:(ts_ + 1) * 1024],
                                ps[:, 0:1024])
                    P.op("vector", b_y,
                         waits=[(st, v_B), (sv, v_comb), (dy[ts_ % 8], yinit_dx[ts_])],
                         inc=sv, tag=f"yev{H}_{seg}_{ts_}")
                    yslot_free[sl] = sv.n
                    if seg == nseg - 1:
                        v_yev = sv.n

                        def b_out(e, H=H, ts_=ts_):
                            r0 = H * S + ts_ * 128
                            return e.dma_start(out_t.ap()[r0:r0 + 128, :],
                                               sb["y_acc"][:, ts_ * 1024:(ts_ + 1) * 1024])
                        P.op("sync", b_out, waits=[(sv, v_yev)], inc=dout, amt=16,
                             tag=f"out{H}_{ts_}")
            v_y_all = sv.n
            v_prev_half_st = st.n
            v_out_prev = dout.n

    if dump is not None:
        if dump in ("stats", "gstats", "comb", "lgbuf"):
            w = sb[dump].ap().shape[1]
            def b_dump(e, dump=dump, w=w):
                return e.dma_start(out_t.ap()[0:128, 0:w], sb[dump][:, 0:w])
        else:
            def b_dump(e, dump=dump):
                return e.dma_start(
                    bass.AP(out_t, 0, [[D, 128], [128 * D, 16], [1, D]]),
                    sb[dump][:, :].rearrange("p (t d) -> p t d", t=16))
        P.op("gpsimd", b_dump,
             waits=[(ss, ss.n), (sv, sv.n), (st, st.n)]
                   + [(s, s.n) for s in dht + dstr + dnat + dx + dy]
                   + [(dsp[0], dsp[0].n), (dsp[1], dsp[1].n)],
             inc=dout, amt=16, tag="dump")

    P.op("sync", None, waits=[(dsp[0], dsp[0].n), (dsp[1], dsp[1].n),
                              (dout, dout.n)], tag="final_wait")

    P.check()

    import contextlib
    with contextlib.ExitStack() as ctx:
        for s in P.sems:
            s.h = ctx.enter_context(nc.semaphore(s.name))
        with nc.Block() as block:
            for eng_name in Plan.ENGS:
                ops = P.ops[eng_name]

                def body(e, ops=ops):
                    for f in ops:
                        f(e)
                getattr(block, eng_name)(body)
    probe = nc.alloc_sbuf_tensor("_guard_probe", [128, 4], mybir.dt.uint8)
    paddr = nc.lookup_mloc(probe).addr
    assert paddr + 4 <= SB0, f"const pool collided with kernel map: {paddr} >= {SB0}"
    return nc


_NC_CACHE = {}


def _get_nc(key="full", **kw):
    if key not in _NC_CACHE:
        _NC_CACHE[key] = _build_program(**kw)
    return _NC_CACHE[key]


def kernel(L0_attn_qz, L0_attn_ternary_factor_u, L0_attn_ternary_factor_v,
           L0_ffn_gate_w, L0_ffn_w_gate, L0_ffn_w_up, L0_ffn_w_down,
           L0_ffn_sw_gate, L0_ffn_sw_up, L0_ffn_sw_down):
    qz = np.ascontiguousarray(np.asarray(L0_attn_qz, dtype=np.float32))
    u = np.ascontiguousarray(np.asarray(L0_attn_ternary_factor_u, dtype=np.float32))
    v = np.ascontiguousarray(np.asarray(L0_attn_ternary_factor_v, dtype=np.float32))
    gw = np.ascontiguousarray(np.asarray(L0_ffn_gate_w, dtype=np.float32))
    wg = np.ascontiguousarray(np.asarray(L0_ffn_w_gate, dtype=np.float32))
    wu = np.ascontiguousarray(np.asarray(L0_ffn_w_up, dtype=np.float32))
    wd = np.ascontiguousarray(np.asarray(L0_ffn_w_down, dtype=np.float32))
    swg = np.ascontiguousarray(np.asarray(L0_ffn_sw_gate, dtype=np.float32))
    swu = np.ascontiguousarray(np.asarray(L0_ffn_sw_up, dtype=np.float32))
    swd = np.ascontiguousarray(np.asarray(L0_ffn_sw_down, dtype=np.float32))

    nc = _get_nc()
    in_maps = []
    for i in range(NCORE):
        xs = np.ascontiguousarray(qz[BPC * i: BPC * (i + 1)].reshape(T, D))
        in_maps.append(dict(x=xs, u=u, v=v, gw=gw, wg=wg, wu=wu, wd=wd,
                            swg=swg, swu=swu, swd=swd))
    res = run_bass_kernel_spmd(nc, in_maps, core_ids=list(range(NCORE)))
    shards = [res.results[i]["out"].reshape(BPC, S, D) for i in range(NCORE)]
    return np.concatenate(shards, axis=0)
